# version 32
# baseline (speedup 1.0000x reference)
import numpy as np
import concourse.bass as bass
import concourse.mybir as mybir
from concourse.bass_utils import run_bass_kernel_spmd

F32 = mybir.dt.float32
BF16 = mybir.dt.bfloat16
AF = mybir.ActivationFunctionType
ALU = mybir.AluOpType
AX = mybir.AxisListType

D = 1024
NH = 16
HD = 64
F = 2048
NE = 2
EPS = 1e-5

ENGINES = ("tensor", "vector", "scalar", "gpsimd", "sync")


class _Op:
    __slots__ = ("eng", "fn", "reads", "writes", "is_dma", "inc", "sem_val",
                 "deps", "waits", "snap", "tag", "slot", "gen")

    def __init__(self, eng, fn, reads, writes, is_dma, tag):
        self.eng = eng
        self.fn = fn
        self.reads = tuple(reads)
        self.writes = tuple(writes)
        self.is_dma = is_dma
        self.inc = is_dma
        self.sem_val = None
        self.deps = set()
        self.waits = []
        self.snap = None
        self.tag = tag


class Builder:
    def __init__(self):
        self.ops = {e: [] for e in ENGINES}
        self.glob = []
        self.state = {}

    def add(self, eng, fn, r=(), w=(), dma=False, tag=""):
        rpb = tuple(k for k in r if k.startswith("pb"))
        if rpb:
            r = tuple(k for k in r if not k.startswith("pb"))
            w = tuple(w) + rpb
        op = _Op(eng, fn, r, w, dma, tag)
        idx = len(self.ops[eng])
        deps = set()
        for k in op.reads:
            st = self.state.get(k)
            if st and st["w"] is not None:
                deps.add(st["w"])
        for k in op.writes:
            st = self.state.get(k)
            if st:
                if st["w"] is not None:
                    deps.add(st["w"])
                deps.update(st["r"])
        op.deps = {d for d in deps if not (d[0] == eng == "tensor" and not dma
                                           and not self.ops[eng][d[1]].is_dma)}
        self.ops[eng].append(op)
        self.glob.append(op)
        ref = (eng, idx)
        for k in op.reads:
            self.state.setdefault(k, {"w": None, "r": []})["r"].append(ref)
        for k in op.writes:
            self.state[k] = {"w": ref, "r": []}
        return op

    def finalize(self, nc):
        R = 8
        self.R = R
        for op in self.glob:
            for (e, j) in op.deps:
                self.ops[e][j].inc = True
        sems = {}
        for e in ENGINES:
            sems[("c", e)] = nc.semaphore(f"sc_{e}").__enter__()
        for e in ("gpsimd", "sync", "scalar"):
            for sl in range(R):
                sems[("q", e, sl)] = nc.semaphore(f"q_{e}_{sl}").__enter__()
        cum = {e: 0 for e in ENGINES}
        qcnt = {e: 0 for e in ENGINES}
        for e in ENGINES:
            for op in self.ops[e]:
                if op.is_dma:
                    i = qcnt[e]
                    qcnt[e] = i + 1
                    op.slot = (i % R)
                    op.gen = (i // R)
                    op.sem_val = 16 * (op.gen + 1)
                elif op.inc:
                    cum[e] += 1
                    op.sem_val = cum[e]
        self.final_counts = {("c", e): cum[e] for e in ENGINES}
        self.qcnt = qcnt
        wm = {e: {} for e in ENGINES}

        def semkey(op):
            if op.is_dma:
                return ("q", op.eng, op.slot)
            return ("c", op.eng)

        for op in self.glob:
            e = op.eng
            needed = []
            for (e2, j2) in op.deps:
                p = self.ops[e2][j2]
                k = semkey(p)
                if wm[e].get(k, 0) >= p.sem_val:
                    continue
                needed.append((k, p.sem_val, p))
            if op.is_dma and op.gen > 0:
                k = ("q", e, op.slot)
                if wm[e].get(k, 0) < 16 * op.gen:
                    needed.append((k, 16 * op.gen, None))
            for (k, v, p) in sorted(needed, key=lambda t: (t[0], t[1])):
                if wm[e].get(k, 0) >= v:
                    continue
                op.waits.append((k, v))
                wm[e][k] = v
                if p is not None and p.snap:
                    for k2, v2 in p.snap.items():
                        if wm[e].get(k2, 0) < v2:
                            wm[e][k2] = v2
            if op.inc:
                snap = dict(wm[e])
                k = semkey(op)
                snap[k] = max(snap.get(k, 0), op.sem_val)
                op.snap = snap
        self.sems = sems

    def emit(self, nc):
        sems = self.sems
        with nc.Block() as block:
            for e in ENGINES:
                ops = self.ops[e]

                def make(eng_ops, ename):
                    def run(eng):
                        for op in eng_ops:
                            for (k, v) in op.waits:
                                eng.wait_ge(sems[k], v)
                            inst = op.fn(eng)
                            if op.is_dma:
                                inst.then_inc(sems[("q", ename, op.slot)], 16)
                            elif op.inc:
                                inst.then_inc(sems[("c", ename)], 1)
                    return run

                getattr(block, e)(make(ops, e))


def _masks():
    p = np.arange(128)[:, None]
    j = np.arange(128)[None, :]
    low = (p <= j).astype(np.float32)
    up = (p >= j).astype(np.float32)
    col0 = np.broadcast_to((p == 0), (128, 128)).astype(np.float32)
    one = np.ones((128, 1), np.float32)
    mA = np.ascontiguousarray(np.concatenate([np.maximum(low, col0), one], axis=1))
    mB = np.ascontiguousarray(np.maximum(up, col0))
    mC = np.ascontiguousarray(np.concatenate([col0, up, low, one], axis=1))
    return mA, mB, mC


def build_program(T=2048):
    NT = T // 128
    NC512 = T // 512
    nc = bass.Bass("TRN2", target_bir_lowering=False)
    B = Builder()
    add = B.add

    dx = nc.dram_tensor("x", [T, D], F32, kind="ExternalInput").ap()
    dWq = nc.dram_tensor("Wq", [D, D], F32, kind="ExternalInput").ap()
    dbq = nc.dram_tensor("bq", [1, D], F32, kind="ExternalInput").ap()
    dWk = nc.dram_tensor("Wk", [D, D], F32, kind="ExternalInput").ap()
    dbk = nc.dram_tensor("bk", [1, D], F32, kind="ExternalInput").ap()
    dWv = nc.dram_tensor("Wv", [D, D], F32, kind="ExternalInput").ap()
    dbv = nc.dram_tensor("bv", [1, D], F32, kind="ExternalInput").ap()
    dWo = nc.dram_tensor("Wo", [D, D], F32, kind="ExternalInput").ap()
    dbo = nc.dram_tensor("bo", [1, D], F32, kind="ExternalInput").ap()
    dWr = nc.dram_tensor("Wr", [D, NE], F32, kind="ExternalInput").ap()
    drb = nc.dram_tensor("rb", [1, NE], F32, kind="ExternalInput").ap()
    dWg = [nc.dram_tensor(f"Wg{e}", [D, F], F32, kind="ExternalInput").ap() for e in range(NE)]
    dWu = [nc.dram_tensor(f"Wu{e}", [D, F], F32, kind="ExternalInput").ap() for e in range(NE)]
    dWd = [nc.dram_tensor(f"Wd{e}", [F, D], F32, kind="ExternalInput").ap() for e in range(NE)]
    dSg = nc.dram_tensor("Sg", [D, F], F32, kind="ExternalInput").ap()
    dSu = nc.dram_tensor("Su", [D, F], F32, kind="ExternalInput").ap()
    dSd = nc.dram_tensor("Sd", [F, D], F32, kind="ExternalInput").ap()
    dmA = nc.dram_tensor("mA", [128, 129], F32, kind="ExternalInput").ap()
    dmB = nc.dram_tensor("mB", [128, 128], F32, kind="ExternalInput").ap()
    dmC = nc.dram_tensor("mC", [128, 385], F32, kind="ExternalInput").ap()
    dident = nc.dram_tensor("ident", [128, 128], F32, kind="ExternalInput").ap()
    dout = nc.dram_tensor("out", [T, D], F32, kind="ExternalOutput").ap()
    dpscr = nc.dram_tensor("pscr", [T, D], BF16, kind="Internal").ap()
    dascr = nc.dram_tensor("ascr", [T, D], BF16, kind="Internal").ap()

    S = lambda n, sh, dt: nc.alloc_sbuf_tensor(n, sh, dt).ap()
    xnT = S("xnT", [128, 8 * T], BF16)
    BIGC = max(16 * T + NT * 66 * NH, 16 * F + 2 * 16 * 512)
    big = S("big", [128, BIGC], BF16)
    QT_OFF, KT_OFF, V_OFF = 0, 8 * T, 16 * T
    H_OFF, WDC_OFF, WDC2_OFF = 0, 16 * F, 16 * F + 8192
    ovl = S("ovl", [128, 12288], BF16)
    WN_OFF, WT_OFF = 0, 4096
    stx = [S("stx0", [128, 1024], F32)]
    stg = [S(f"stg{i}", [128, 1024], BF16) for i in range(4)]
    _sto = S("sto", [128, 1024], F32)
    sto = [_sto[:, 0:512], _sto[:, 512:1024]]
    stxr = S("stxr", [128, 1024], F32)
    wbc = [S(f"wbc{e}", [128, T], BF16) for e in range(NE)]
    exT = [S(f"exT{i}", [128, 385], BF16) for i in range(6)]
    a_nat = S("a_nat", [128, 1024], BF16)
    a_T = S("a_T", [128, 1024], BF16)
    mskA = S("mskA", [128, 129], BF16)
    mskB = S("mskB", [128, 128], BF16)
    mskC = S("mskC", [128, 385], BF16)
    ident = S("ident_s", [128, 128], BF16)
    bias_qkv = S("bias_qkv", [1, D], BF16)
    bias_qkv2 = S("bias_qkv2", [1, D], BF16)
    bias_o = S("bias_o", [1, D], BF16)
    bqkT = S("bqkT", [128, 16], F32)
    r0bf = S("r0bf", [128, 448], BF16)
    ones = S("ones_s", [1, 512], BF16)
    stat = S("stat", [128, 40], F32)
    kons = S("kons", [128, 1], F32)
    xnb = S("xnb", [128, D], BF16)
    wrc = S("wrc", [128, 16], F32)
    wrdf = S("wrdf", [128, 8], F32)
    wrdH = S("wrdH", [128, 8], BF16)
    wrdL = S("wrdL", [128, 8], BF16)
    wrow = S("wrow", [1, 1024], BF16)
    dhi = S("dhi", [128, 16], F32)
    thrc = S("thrc", [128, 1], F32)
    rbs = S("rbs", [1, 2], F32)
    rbt = S("rbt", [1, 2], F32)
    rbtb = S("rbtb", [1, 2], BF16)
    rtr = S("rtr", [128, 16 * 5], F32)
    w0c = S("w0c", [128, 16], F32)
    w1c = S("w1c", [128, 16], F32)
    wcb = S("wcb", [128, 16], BF16)
    rsb = S("rsb", [128, 16], F32)
    r0tmp = S("r0tmp", [128, 468], F32)
    wrdbc = wbc[0][:, 0:1024] if T >= 1024 else S("wrdbc", [128, 1024], BF16)[:]
    stp = stg[0][:, 0:512]

    P = lambda n, sh, dt: nc.alloc_psum_tensor(n, sh, dt).ap()
    pb = [P(f"pb{i}", [128, 512], F32) for i in range(6)]
    pbU = P("pbU", [128, 1024], BF16)
    pbT = P("pbT", [128, 1024], BF16)
    PSTEP_B = big.ap[0][0]
    PSTEP_T = pbT.ap[0][0]

    def big_keys(lo_col, hi_col):
        return tuple(f"big:{c}" for c in range(lo_col // 2048, (hi_col + 2047) // 2048))

    qt_k = lambda it: big_keys(QT_OFF + T * it, QT_OFF + T * (it + 1))
    kt_k = lambda it: big_keys(KT_OFF + T * it, KT_OFF + T * (it + 1))
    v_k = lambda ntl: big_keys(V_OFF + 66 * NH * ntl, V_OFF + 66 * NH * (ntl + 1))
    h_k = lambda f: big_keys(H_OFF + T * f, H_OFF + T * (f + 1))

    wn_k = ("ovl:wn",)
    wtI = lambda ti: f"wtI{ti}"

    add("gpsimd", lambda g: g.dma_start(out=ident[:], in_=dident), w=["ident"], dma=True)
    add("gpsimd", lambda g: g.dma_start(out=mskA[:], in_=dmA), w=["mskA"], dma=True)
    add("gpsimd", lambda g: g.dma_start(out=mskB[:], in_=dmB), w=["mskB"], dma=True)
    add("gpsimd", lambda g: g.dma_start(out=mskC[:], in_=dmC), w=["mskC"], dma=True)
    add("vector", lambda v: v.memset(ones[:], 1.0), w=["ones"])
    add("vector", lambda v: v.memset(kons[:, 0:1], EPS), w=["kons"])
    vones = bass.AP(big.tensor, V_OFF + 64, [[PSTEP_B, 128], [66, NH * NT], [1, 1]])
    add("vector", lambda v: v.memset(vones, 1.0),
        w=list(big_keys(V_OFF, V_OFF + 66 * NH * NT)))

    add("sync", lambda s: s.dma_start(out=rbs[:], in_=drb), w=["rbs"], dma=True)
    add("vector", lambda v: v.tensor_sub(rbt[0:1, 0:1], rbs[0:1, 1:2], rbs[0:1, 0:1]), r=["rbs"], w=["rbt0"])
    add("vector", lambda v: v.tensor_copy(rbtb[0:1, 0:1], rbt[0:1, 0:1]), r=["rbt0"], w=["rbtb0"])
    add("vector", lambda v: v.tensor_sub(rbt[0:1, 1:2], rbt[0:1, 0:1], rbtb[0:1, 0:1]), r=["rbt0", "rbtb0"], w=["rbt1"])
    add("vector", lambda v: v.tensor_copy(rbtb[0:1, 1:2], rbt[0:1, 1:2]), r=["rbt1"], w=["rbtb1"])
    add("sync", lambda s: s.dma_start(out=wrc.rearrange("p (t e) -> p t e", t=8),
                                      in_=dWr.rearrange("(t p) e -> p t e", p=128)), w=["wrc"], dma=True)
    add("vector", lambda v: v.tensor_sub(wrdf[:], wrc[:, 0:16:2], wrc[:, 1:16:2]), r=["wrc"], w=["wrdf"])
    add("vector", lambda v: v.tensor_copy(wrdH[:], wrdf[:]), r=["wrdf"], w=["wrdH"])
    add("vector", lambda v: v.tensor_sub(wrdL[:], wrdf[:], wrdH[:]), r=["wrdf", "wrdH"], w=["wrdL"])
    add("tensor", lambda t: t.matmul(pb[1][:, 32:33], ones[0:1, 0:128], rbtb[0:1, 0:1], start=True, stop=False),
        r=["rbtb0", "ones"], w=["pb1"])
    add("tensor", lambda t: t.matmul(pb[1][:, 32:33], ones[0:1, 0:128], rbtb[0:1, 1:2], start=False, stop=True),
        r=["rbtb1", "ones"], w=["pb1"])
    add("vector", lambda v: v.tensor_copy(thrc[:], pb[1][:, 32:33]), r=["pb1"], w=["thrc"])
    for t_ in range(8):
        add("tensor", (lambda t_=t_: (lambda tt: tt.transpose(
            pbT[0:1, 0:128], wrdH[:, t_:t_ + 1], ident[:])))(), r=["wrdH", "ident"], w=["pbT"])
        add("vector", (lambda t_=t_: (lambda v: v.tensor_copy(
            wrow[0:1, 128 * t_:128 * (t_ + 1)], pbT[0:1, 0:128])))(), r=["pbT"], w=["wrow"])
    for c in range(2):
        add("tensor", (lambda c=c: (lambda t: t.matmul(
            pb[1][:, 0:512], ones[0:1, 0:128], wrow[0:1, 512 * c:512 * (c + 1)],
            start=True, stop=True)))(), r=["wrow", "ones"], w=["pb1"])
        add("vector", (lambda c=c: (lambda v: v.tensor_copy(
            wrdbc[:, 512 * c:512 * (c + 1)], pb[1][:, 0:512])))(), r=["pb1"],
            w=["wbc0" if T >= 1024 else "wrdbc"])

    def load_w_quarter(dW, q, tag):
        qb_ = 2048 * (q % 2)
        add("gpsimd", (lambda dW=dW, q=q, qb_=qb_: (lambda g: g.dma_start(
            out=ovl[:, qb_:qb_ + 2048].rearrange("p (a b) -> p a b", a=2),
            in_=dW.rearrange("(t p) d -> p t d", p=128)[:, 2 * q:2 * (q + 1), :])))(),
            w=[f"wn{q % 2}"], dma=True, tag=tag)

    def transpose_w_quarter(q, tag):
        opst = ovl.ap[0][0]
        qb_ = 2048 * (q % 2)
        for t_il in range(2):
            t_i = 2 * q + t_il
            for g4 in range(2):
                par = (t_il * 2 + g4) % 2
                dst = pbT if par == 0 else pbU
                dkey = "pbT" if par == 0 else "pbU"
                for t_d4 in range(4):
                    t_d = 4 * g4 + t_d4
                    add("tensor", (lambda t_il=t_il, t_d=t_d, t_d4=t_d4, qb_=qb_, dst=dst: (lambda t: t.transpose(
                        dst[:, 128 * t_d4:128 * (t_d4 + 1)],
                        ovl[:, qb_ + 1024 * t_il + 128 * t_d:qb_ + 1024 * t_il + 128 * (t_d + 1)],
                        ident[:])))(), r=[f"wn{q % 2}", "ident"], w=[dkey])
                if par == 0:
                    add("scalar", (lambda t_i=t_i, g4=g4, opst=opst: (lambda s: s.copy(
                        bass.AP(ovl.tensor, WT_OFF + 1024 * 4 * g4 + 128 * t_i,
                                [[opst, 128], [1024, 4], [1, 128]]),
                        pbT[:, 0:512].rearrange("p (a b) -> p a b", a=4))))(),
                        r=["pbT"], w=[wtI(t_i)])
                else:
                    add("vector", (lambda t_i=t_i, g4=g4, opst=opst: (lambda v: v.tensor_copy(
                        bass.AP(ovl.tensor, WT_OFF + 1024 * 4 * g4 + 128 * t_i,
                                [[opst, 128], [1024, 4], [1, 128]]),
                        pbU[:, 0:512].rearrange("p (a b) -> p a b", a=4))))(),
                        r=["pbU"], w=[wtI(t_i)])

    def gen_qk_one(dstoff, dst_keys_fn, bcol, nck, it, bank, eng):
        pkey = f"pb{bank}"
        for td in range(8):
            add("tensor", (lambda it=it, nck=nck, td=td, bank=bank: (lambda t: t.matmul(
                pb[bank][:, 0:512],
                ovl[:, WT_OFF + 1024 * td + 128 * it: WT_OFF + 1024 * td + 128 * (it + 1)],
                xnT[:, T * td + 512 * nck: T * td + 512 * (nck + 1)],
                start=(td == 0), stop=(td == 7))))(), r=[wtI(it), "xnT"], w=[pkey])
        if eng == "scalar":
            add("scalar", (lambda it=it, nck=nck, bank=bank, dstoff=dstoff, bcol=bcol: (lambda s: s.activation(
                bass.AP(big.tensor, dstoff + T * it + 512 * nck, [[PSTEP_B, 128], [1, 512]]),
                pb[bank][:, 0:512], AF.Identity,
                bias=bqkT[:, bcol + it:bcol + it + 1], scale=1.0)))(),
                r=[pkey, "bqkT"], w=list(dst_keys_fn(it)))
        else:
            add("vector", (lambda it=it, nck=nck, bank=bank, dstoff=dstoff, bcol=bcol: (lambda v: v.tensor_tensor(
                bass.AP(big.tensor, dstoff + T * it + 512 * nck, [[PSTEP_B, 128], [1, 512]]),
                pb[bank][:, 0:512],
                bass.AP(bqkT.tensor, bcol + it, [[bqkT.ap[0][0], 128], [0, 512]]),
                ALU.add)))(), r=[pkey, "bqkT"], w=list(dst_keys_fn(it)))

    def gen_qkt_nck(dstoff, dst_keys_fn, bcol, nck):
        for it in range(8):
            gen_qk_one(dstoff, dst_keys_fn, bcol, nck, it,
                       [5, 4][it % 2], "scalar" if it % 2 == 0 else "vector")

    def gen_kt_itpair(it0):
        for it in (it0, it0 + 1):
            for nck in range(NC512):
                gen_qk_one(KT_OFF, kt_k, 8, nck, it,
                           [5, 4][nck % 2], "scalar" if nck % 2 == 0 else "vector")

    def vgen_one(ntl, ih):
        wkeys = [wtI(4 * ih + j) for j in range(4)]
        bank = [5, 4][(ntl * 2 + ih) % 2]
        pkey = f"pb{bank}"
        for td in range(8):
            add("tensor", (lambda ntl=ntl, ih=ih, td=td, bank=bank: (lambda t: t.matmul(
                pb[bank][:, 0:512],
                xnT[:, T * td + 128 * ntl: T * td + 128 * (ntl + 1)],
                ovl[:, WT_OFF + 1024 * td + 512 * ih: WT_OFF + 1024 * td + 512 * (ih + 1)],
                start=(td == 0), stop=(td == 7))))(), r=wkeys + ["xnT"], w=[pkey])
        add("vector", (lambda ntl=ntl, ih=ih, bank=bank: (lambda v: v.tensor_tensor(
            bass.AP(big.tensor, V_OFF + 66 * NH * ntl + 66 * 8 * ih,
                    [[PSTEP_B, 128], [66, 8], [1, 64]]),
            pb[bank].rearrange("p (h d) -> p h d", h=8),
            bass.AP(a_nat.tensor, 512 * ih, [[a_nat.ap[0][0], 128], [64, 8], [1, 64]]),
            ALU.add)))(), r=[pkey, "a_nat"], w=list(v_k(ntl)))

    def load_transpose_w(dW, tag):
        load_w_quarter(dW, 0, tag)
        load_w_quarter(dW, 1, tag)
        transpose_w_quarter(0, tag)
        load_w_quarter(dW, 2, tag)
        transpose_w_quarter(1, tag)
        load_w_quarter(dW, 3, tag)
        transpose_w_quarter(2, tag)
        transpose_w_quarter(3, tag)

    add("gpsimd", lambda g: g.dma_start(out=bias_qkv[:], in_=dbq), w=["bias_qkv"], dma=True)
    load_transpose_w(dWq, "Wq")
    add("gpsimd", lambda g: g.dma_start(out=bias_qkv2[:], in_=dbk), w=["bias_qkv2"], dma=True)
    for bi_, (brow, brk) in enumerate(((bias_qkv, "bias_qkv"), (bias_qkv2, "bias_qkv2"))):
        for it in range(8):
            add("tensor", (lambda bi_=bi_, it=it, brow=brow, brk=brk: (lambda t: t.matmul(
                pb[1][:, 8 * bi_ + it:8 * bi_ + it + 1], brow[0:1, 128 * it:128 * (it + 1)],
                ones[0:1, 0:1], start=True, stop=True)))(), r=[brk, "ones"], w=["pb1"])
    add("vector", lambda v: v.tensor_copy(bqkT[:, 0:16], pb[1][:, 0:16]), r=["pb1"], w=["bqkT"])
    load_w_quarter(dWk, 0, "Wk")

    def p0_s1(nt):
        par = nt % 2
        sx = stx[0] if par == 0 else _sto
        kxl = ["stx0"] if par == 0 else ["sto0", "sto1"]
        xnb_t = xnb if par == 0 else a_T[:, 0:D]
        kb = "xnb" if par == 0 else "a_T"
        sb_ = 20 * par
        ks = lambda i: f"st{i}:{par}"
        stc = lambda i, n=1: stat[:, sb_ + i:sb_ + i + n]
        add("sync", (lambda nt=nt, sx=sx: (lambda s: s.dma_start(
            out=sx[:, 0:D], in_=dx[128 * nt:128 * (nt + 1), :])))(), w=kxl, dma=True)
        add("vector", (lambda sx=sx, stc=stc: (lambda v: v.bn_stats(
            stc(0, 6), sx[:, 0:512])))(), r=kxl, w=[ks(0)])
        add("vector", (lambda sx=sx, stc=stc: (lambda v: v.bn_stats(
            stc(6, 6), sx[:, 512:1024])))(), r=kxl, w=[ks(106)])
        add("vector", (lambda stc=stc: (lambda v: v.bn_aggr(stc(12, 2), stc(0, 12))))(),
            r=[ks(0), ks(106)], w=[ks(6)])
        add("scalar", (lambda stc=stc: (lambda s: s.activation(stc(14), stc(13), AF.Ln, bias=kons[:, 0:1], scale=1.0)))(),
            r=[ks(6), "kons"], w=[ks(14)])
        add("scalar", (lambda stc=stc: (lambda s: s.activation(stc(14), stc(14), AF.Exp, scale=-0.5)))(),
            r=[ks(14)], w=[ks(14) + "b"])
        add("vector", (lambda stc=stc: (lambda v: v.scalar_tensor_tensor(
            out=stc(15), in0=stc(12), scalar=-1.0, in1=stc(14), op0=ALU.mult, op1=ALU.mult)))(),
            r=[ks(6), ks(14) + "b"], w=[ks(15)])
        add("scalar", (lambda sx=sx, stc=stc, xnb_t=xnb_t, kxl=kxl: (lambda s: s.activation(
            xnb_t[:], sx[:, 0:D], AF.Identity, bias=stc(15), scale=stc(14))))(),
            r=kxl + [ks(15), ks(14) + "b"], w=[kb])

    def p0_s2(nt):
        par = nt % 2
        xnb_t = xnb if par == 0 else a_T[:, 0:D]
        kb = "xnb" if par == 0 else "a_T"
        for td in range(8):
            add("tensor", (lambda td=td, xnb_t=xnb_t, kb=kb: (lambda t: t.transpose(
                pbT[:, 128 * td:128 * (td + 1)], xnb_t[:, 128 * td:128 * (td + 1)], ident[:])))(),
                r=[kb, "ident"], w=["pbT"])
        add("vector", (lambda nt=nt: (lambda v: v.tensor_copy(
            bass.AP(xnT.tensor, 128 * nt, [[xnT.ap[0][0], 128], [T, 8], [1, 128]]),
            pbT.rearrange("p (t c) -> p t c", t=8))))(), r=["pbT"], w=["xnT"])
        if nt % 4 == 3:
            for nt2 in range(nt - 3, nt + 1):
                for td in range(8):
                    for hl, wv, wk_ in ((0, wrdH, "wrdH"), (1, wrdL, "wrdL")):
                        add("tensor", (lambda nt2=nt2, td=td, wv=wv, hl=hl: (lambda t: t.matmul(
                            pb[1][:, nt2:nt2 + 1],
                            xnT[:, T * td + 128 * nt2: T * td + 128 * (nt2 + 1)],
                            wv[:, td:td + 1], start=(td == 0 and hl == 0), stop=(td == 7 and hl == 1))))(),
                            r=["xnT", wk_], w=["pb1"])
            add("vector", (lambda nt=nt: (lambda v: v.tensor_copy(
                dhi[:, nt - 3:nt + 1], pb[1][:, nt - 3:nt + 1])))(), r=["pb1"],
                w=[f"dhi:{i}" for i in range(nt - 3, nt + 1)])
            gen_qkt_nck(QT_OFF, qt_k, 0, nt // 4)

    p0_s1(0)
    for nt in range(1, NT):
        p0_s1(nt)
        p0_s2(nt - 1)
    p0_s2(NT - 1)

    load_w_quarter(dWk, 1, "Wk")
    transpose_w_quarter(0, "WkT")
    load_w_quarter(dWk, 2, "Wk")
    gen_kt_itpair(0)
    transpose_w_quarter(1, "WkT")
    load_w_quarter(dWk, 3, "Wk")
    gen_kt_itpair(2)
    transpose_w_quarter(2, "WkT")
    load_w_quarter(dWv, 0, "Wv")
    gen_kt_itpair(4)
    transpose_w_quarter(3, "WkT")
    load_w_quarter(dWv, 1, "Wv")
    gen_kt_itpair(6)
    add("gpsimd", lambda g: g.dma_start(out=bias_qkv[:], in_=dbv), w=["bias_qkv"], dma=True)
    transpose_w_quarter(0, "WvT")
    load_w_quarter(dWv, 2, "Wv")
    transpose_w_quarter(1, "WvT")
    load_w_quarter(dWv, 3, "Wv")
    for ih in range(2):
        add("tensor", (lambda ih=ih: (lambda t: t.matmul(
            pb[1][:, 0:512], ones[0:1, 0:128],
            bias_qkv[0:1, 512 * ih:512 * (ih + 1)], start=True, stop=True)))(),
            r=["bias_qkv", "ones"], w=["pb1"])
        add("vector", (lambda ih=ih: (lambda v: v.tensor_copy(
            a_nat[:, 512 * ih:512 * (ih + 1)], pb[1][:, 0:512])))(), r=["pb1"], w=["a_nat"])
    for ntl in range(8):
        vgen_one(ntl, 0)
    transpose_w_quarter(2, "WvT")
    load_w_quarter(dWo, 0, "Wo")
    for ntl in range(8, NT):
        vgen_one(ntl, 0)
    transpose_w_quarter(3, "WvT")
    load_w_quarter(dWo, 1, "Wo")
    for ntl in range(8):
        vgen_one(ntl, 1)
    transpose_w_quarter(0, "WoT")
    load_w_quarter(dWo, 2, "Wo")
    for ntl in range(8, NT):
        vgen_one(ntl, 1)
    transpose_w_quarter(1, "WoT")
    load_w_quarter(dWo, 3, "Wo")
    transpose_w_quarter(2, "WoT")
    transpose_w_quarter(3, "WoT")
    add("gpsimd", lambda g: g.dma_start(out=bias_o[:], in_=dbo), w=["bias_o"], dma=True)

    RT = NT
    dhi_keys = [f"dhi:{i}" for i in range(NT)]
    add("vector", lambda v: v.tensor_copy(rtr[:, 0:RT], dhi[:, 0:RT]),
        r=dhi_keys, w=["delta"])
    add("scalar", lambda s: s.activation(rtr[:, 16:16 + RT], rtr[:, 0:RT], AF.Exp, scale=-1.0),
        r=["delta"], w=["re"])
    add("vector", lambda v: v.tensor_scalar_add(rtr[:, 16:16 + RT], rtr[:, 16:16 + RT], 1.0), r=["re"], w=["re1"])
    add("vector", lambda v: v.reciprocal(rtr[:, 32:32 + RT], rtr[:, 16:16 + RT]), r=["re1"], w=["s0"])
    add("vector", lambda v: v.tensor_scalar(out=rtr[:, 48:48 + RT], in0=rtr[:, 0:RT], scalar1=thrc[:, 0:1],
                                            scalar2=None, op0=ALU.is_ge), r=["delta", "thrc"], w=["h0"])
    add("vector", lambda v: v.tensor_add(rtr[:, 64:64 + RT], rtr[:, 48:48 + RT], rtr[:, 32:32 + RT]),
        r=["h0", "s0"], w=["t0"])
    add("vector", lambda v: v.tensor_sub(rtr[:, 64:64 + RT], rtr[:, 64:64 + RT], rtr[:, 32:32 + RT]),
        r=["t0", "s0"], w=["t0b"])
    add("vector", lambda v: v.tensor_mul(w0c[:, 0:RT], rtr[:, 48:48 + RT], rtr[:, 64:64 + RT]),
        r=["h0", "t0b"], w=["w0c"])
    add("vector", lambda v: v.tensor_scalar(out=rtr[:, 48:48 + RT], in0=rtr[:, 48:48 + RT], scalar1=-1.0,
                                            scalar2=1.0, op0=ALU.mult, op1=ALU.add), r=["h0"], w=["h1"])
    add("vector", lambda v: v.tensor_scalar(out=rtr[:, 32:32 + RT], in0=rtr[:, 32:32 + RT], scalar1=-1.0,
                                            scalar2=1.0, op0=ALU.mult, op1=ALU.add), r=["s0"], w=["s1"])
    add("vector", lambda v: v.tensor_add(rtr[:, 64:64 + RT], rtr[:, 48:48 + RT], rtr[:, 32:32 + RT]),
        r=["h1", "s1"], w=["t1"])
    add("vector", lambda v: v.tensor_sub(rtr[:, 64:64 + RT], rtr[:, 64:64 + RT], rtr[:, 32:32 + RT]),
        r=["t1", "s1"], w=["t1b"])
    add("vector", lambda v: v.tensor_mul(w1c[:, 0:RT], rtr[:, 48:48 + RT], rtr[:, 64:64 + RT]),
        r=["h1", "t1b"], w=["w1c"])
    for (wc, wrk, bco, bck) in ((w0c, "w0c", wbc[0], "wbc0"), (w1c, "w1c", wbc[1], "wbc1")):
        add("vector", (lambda wc=wc: (lambda v: v.tensor_copy(wcb[:, 0:NT], wc[:, 0:NT])))(),
            r=[wrk], w=["wcb"])
        for c in range(T // 512):
            for k4 in range(4):
                nt = 4 * c + k4
                add("tensor", (lambda nt=nt, k4=k4: (lambda t: t.transpose(
                    pbT[0:1, 128 * k4:128 * (k4 + 1)], wcb[:, nt:nt + 1], ident[:])))(),
                    r=["wcb", "ident"], w=["pbT"])
            add("vector", lambda v: v.tensor_copy(wrow[0:1, 0:512], pbT[0:1, 0:512]),
                r=["pbT"], w=["wrow"])
            add("tensor", (lambda c=c: (lambda t: t.matmul(
                pb[1][:, 0:512], ones[0:1, 0:128], wrow[0:1, 0:512],
                start=True, stop=True)))(), r=["wrow", "ones"], w=["pb1"])
            add("vector", (lambda bco=bco, c=c, bck=bck: (lambda v: v.tensor_copy(
                bco[:, 512 * c:512 * (c + 1)], pb[1][:, 0:512])))(), r=["pb1"], w=[bck])

    SBANKS = (0, 1, 5)
    def qb_width(qb):
        return 129 if qb == 0 else (257 if qb == 1 else 385)

    def gu_preload(dW_, f_, tag=""):
        sg = stg[f_ % 4]
        gk = f"stg{f_ % 4}"
        add("gpsimd", (lambda dW_=dW_, f_=f_, sg=sg: (lambda g: g.dma_start(
            out=sg.rearrange("p (t f) -> p t f", t=8),
            in_=dW_.rearrange("(t p) f -> p t f", p=128)[:, :, 128 * f_:128 * (f_ + 1)])))(),
            w=[gk], dma=True, tag=tag)

    add("vector", lambda v: v.memset(r0tmp[:], 0.0), w=["r0tmp"])
    for dh in range(2):
        add("tensor", (lambda dh=dh: (lambda t: t.matmul(
            pb[1][:, 0:512], ones[0:1, 0:128],
            bias_o[0:1, 512 * dh:512 * (dh + 1)], start=True, stop=True)))(),
            r=["bias_o", "ones"], w=["pb1"])
        add("vector", (lambda dh=dh: (lambda v: v.tensor_copy(
            xnb[:, 512 * dh:512 * (dh + 1)], pb[1][:, 0:512])))(), r=["pb1"], w=["xnb"])
    qb_order = list(range(1, NT)) + [0]
    pending_tail = None
    for qi, qb in enumerate(qb_order):
        if qb == 0:
            for pq in (1, 2):
                add("sync", (lambda pq=pq: (lambda sy: sy.dma_start(
                    out=stg[pq % 2][:], in_=dascr[128 * pq:128 * (pq + 1), :])))(),
                    r=[f"ascr:{pq}"], w=[f"stg{pq % 2}"], dma=True)
        Wd_ = qb_width(qb)
        msk = mskA if qb == 0 else (mskB if qb == 1 else mskC)
        mkey = "mskA" if qb == 0 else ("mskB" if qb == 1 else "mskC")
        if qb == 0:
            blocks = [(0, 0)]
        elif qb == 1:
            blocks = [(0, 0), (128, 1)]
        else:
            blocks = [(0, 0), (128, qb - 1), (256, qb)]
        cur_kt = qb

        def produce_pair(tpr):
            h0, h1 = 2 * tpr, 2 * tpr + 1
            for bi, (co, kt) in enumerate(blocks):
                for h in (h0, h1):
                    sb = SBANKS[h % 3]
                    pbase = 64 * (h % 2)
                    tih = h // 2
                    add("tensor", (lambda co=co, kt=kt, sb=sb, pbase=pbase, tih=tih, qb=qb: (lambda t: t.matmul(
                        pb[sb][:, co:co + 128],
                        big[pbase:pbase + 64, KT_OFF + T * tih + 128 * kt: KT_OFF + T * tih + 128 * (kt + 1)],
                        big[pbase:pbase + 64, QT_OFF + T * tih + 128 * qb: QT_OFF + T * tih + 128 * (qb + 1)],
                        start=True, stop=True, tile_position=(pbase, 0))))(),
                        r=list(kt_k(tih)) + list(qt_k(tih)), w=[f"pb{sb}"])
            for h in (h0, h1):
                sb = SBANKS[h % 3]
                pbase = 64 * (h % 2)
                tih = h // 2
                add("tensor", (lambda sb=sb, pbase=pbase, tih=tih, cur_kt=cur_kt, Wd_=Wd_: (lambda t: t.matmul(
                    pb[sb][:, Wd_ - 1:Wd_],
                    big[pbase:pbase + 64, KT_OFF + T * tih + 128 * cur_kt: KT_OFF + T * tih + 128 * (cur_kt + 1)],
                    big[pbase:pbase + 64, QT_OFF + T * tih: QT_OFF + T * tih + 1],
                    start=True, stop=True, tile_position=(pbase, 0))))(),
                    r=list(kt_k(tih)) + list(qt_k(tih)), w=[f"pb{sb}"])
            for h in (h0, h1):
                sb = SBANKS[h % 3]
                add("scalar", (lambda sb=sb, h=h, Wd_=Wd_: (lambda s: s.activation(
                    exT[h % 6][:, 0:Wd_], pb[sb][:, 0:Wd_], AF.Exp, scale=0.125)))(),
                    r=[f"pb{sb}"], w=[f"exT{h % 6}"])
                meng = "vector" if h % 2 == 0 else "gpsimd"
                if qb == 1:
                    add(meng, (lambda h=h: (lambda v: v.tensor_mul(
                        exT[h % 6][:, 0:128], exT[h % 6][:, 0:128], mskB[:, 0:128])))(),
                        r=[f"exT{h % 6}", "mskB"], w=[f"exT{h % 6}"])
                    add(meng, (lambda h=h: (lambda v: v.tensor_mul(
                        exT[h % 6][:, 128:257], exT[h % 6][:, 128:257], mskC[:, 256:385])))(),
                        r=[f"exT{h % 6}", "mskC"], w=[f"exT{h % 6}"])
                else:
                    add(meng, (lambda h=h, msk=msk, Wd_=Wd_: (lambda v: v.tensor_mul(
                        exT[h % 6][:, 0:Wd_], exT[h % 6][:, 0:Wd_], msk[:, 0:Wd_])))(),
                        r=[f"exT{h % 6}", mkey], w=[f"exT{h % 6}"])

        def consume_pair(tpr):
            for h in (2 * tpr, 2 * tpr + 1):
                ob = 2 + (h // 7)
                okey = f"pb{ob}"
                ocol = 65 * (h % 7)
                for bi, (co, kt) in enumerate(blocks):
                    add("tensor", (lambda co=co, kt=kt, bi=bi, h=h, ob=ob, ocol=ocol, nb=len(blocks): (lambda t: t.matmul(
                        pb[ob][:, ocol:ocol + 65], exT[h % 6][:, co:co + 128],
                        big[:, V_OFF + 66 * NH * kt + 66 * h: V_OFF + 66 * NH * kt + 66 * h + 65],
                        start=(bi == 0), stop=(bi == nb - 1))))(),
                        r=[f"exT{h % 6}"] + list(v_k(kt)), w=[okey])
                if h < 15:
                    rb_, rc_ = 32 * (h // 5), 130 + 65 * (h % 5)
                else:
                    rb_, rc_ = 0, 130
                add("tensor", (lambda h=h, rb_=rb_, rc_=rc_, cur_kt=cur_kt, Wd_=Wd_: (lambda t: t.matmul(
                    pb[4][rb_:rb_ + 1, rc_:rc_ + 65],
                    exT[h % 6][:, Wd_ - 1:Wd_],
                    big[:, V_OFF + 66 * NH * cur_kt + 66 * h: V_OFF + 66 * NH * cur_kt + 66 * h + 65],
                    start=True, stop=True)))(), r=[f"exT{h % 6}"] + list(v_k(cur_kt)), w=["pb4"])
            if tpr == 0:
                add("vector", lambda v: v.tensor_add(
                    r0tmp[0:1, 0:65], r0tmp[0:1, 0:65], pb[4][0:1, 130:195]),
                    r=["pb4", "r0tmp"], w=["r0tmp"])

        def evac_bank(ob, nh_, rb0):
            add("vector", (lambda ob=ob, nh_=nh_, rb0=rb0: (lambda v: v.reciprocal(
                rsb[:, rb0:rb0 + nh_],
                bass.AP(pb[ob].tensor, pb[ob].offset + 64, [[pb[ob].ap[0][0], 128], [65, nh_]]))))(),
                r=[f"pb{ob}"], w=["rsb"])
            add("vector", (lambda ob=ob, nh_=nh_, rb0=rb0: (lambda v: v.tensor_tensor(
                bass.AP(a_nat.tensor, 64 * rb0, [[a_nat.ap[0][0], 128], [64, nh_], [1, 64]]),
                bass.AP(pb[ob].tensor, pb[ob].offset, [[pb[ob].ap[0][0], 128], [65, nh_], [1, 64]]),
                bass.AP(rsb.tensor, rb0, [[rsb.ap[0][0], 128], [1, nh_], [0, 64]]),
                ALU.mult)))(), r=[f"pb{ob}", "rsb"], w=["a_nat"])

        produce_pair(0)
        produce_pair(1)
        produce_pair(2)
        if pending_tail is not None:
            pending_tail()
            pending_tail = None
        for tpr in range(3, 8):
            consume_pair(tpr - 3)
            if tpr == 7:
                evac_bank(2, 7, 0)
            produce_pair(tpr)
        consume_pair(5)
        consume_pair(6)
        evac_bank(3, 7, 7)
        consume_pair(7)
        evac_bank(4, 2, 14)
        add("vector", lambda v: v.tensor_add(
            r0tmp[0:1, 65:325], r0tmp[0:1, 65:325], pb[4][0:1, 195:455]),
            r=["pb4", "r0tmp"], w=["r0tmp"])
        add("vector", lambda v: v.tensor_add(
            r0tmp[0:1, 325:390], r0tmp[0:1, 325:390], pb[4][0:1, 130:195]),
            r=["pb4", "r0tmp"], w=["r0tmp"])
        for rb_ in (32, 64):
            add("vector", (lambda rb_=rb_: (lambda v: v.tensor_add(
                r0tmp[rb_:rb_ + 1, 0:325], r0tmp[rb_:rb_ + 1, 0:325],
                pb[4][rb_:rb_ + 1, 130:455])))(), r=["pb4", "r0tmp"], w=["r0tmp"])
        if qb == 0:
            for rb_, ncnt in ((0, 6), (32, 5), (64, 5)):
                add("vector", (lambda rb_=rb_, ncnt=ncnt: (lambda v: v.reciprocal(
                    bass.AP(r0tmp.tensor, 468 * rb_ + 420, [[468, 1], [1, ncnt]]),
                    bass.AP(r0tmp.tensor, 468 * rb_ + 64, [[468, 1], [65, ncnt]]))))(),
                    r=["r0tmp"], w=["r0tmp"])
                add("vector", (lambda rb_=rb_, ncnt=ncnt: (lambda v: v.tensor_tensor(
                    bass.AP(r0bf.tensor, r0bf.ap[0][0] * rb_, [[r0bf.ap[0][0], 1], [64, ncnt], [1, 64]]),
                    bass.AP(r0tmp.tensor, 468 * rb_, [[468, 1], [65, ncnt], [1, 64]]),
                    bass.AP(r0tmp.tensor, 468 * rb_ + 420, [[468, 1], [1, ncnt], [0, 64]]), ALU.mult)))(),
                    r=["r0tmp"], w=["r0bf"])
            add("sync", lambda sy: sy.dma_start(
                out=a_nat[0:1, 0:320], in_=r0bf[0:1, 0:320]),
                r=["r0bf"], w=["a_nat"], dma=True)
            add("sync", lambda sy: sy.dma_start(
                out=a_nat[0:1, 960:1024], in_=r0bf[0:1, 320:384]),
                r=["r0bf"], w=["a_nat"], dma=True)
            add("sync", lambda sy: sy.dma_start(
                out=a_nat[0:1, 320:640], in_=r0bf[32:33, 0:320]),
                r=["r0bf"], w=["a_nat"], dma=True)
            add("sync", lambda sy: sy.dma_start(
                out=a_nat[0:1, 640:960], in_=r0bf[64:65, 0:320]),
                r=["r0bf"], w=["a_nat"], dma=True)
        def make_tail(qb=qb):
            def tail():
                for g4 in range(2):
                    dst = pbT if g4 == 0 else pbU
                    dkey = "pbT" if g4 == 0 else "pbU"
                    for t4 in range(4):
                        ti = 4 * g4 + t4
                        add("tensor", (lambda ti=ti, t4=t4, dst=dst, dkey=dkey: (lambda t: t.transpose(
                            dst[:, 128 * t4:128 * (t4 + 1)],
                            a_nat[:, 128 * ti:128 * (ti + 1)], ident[:])))(),
                            r=["a_nat", "ident"], w=[dkey])
                    if g4 == 0:
                        add("scalar", (lambda g4=g4: (lambda s: s.copy(
                            bass.AP(a_T.tensor, 512 * g4, [[a_T.ap[0][0], 128], [128, 4], [1, 128]]),
                            pbT[:, 0:512].rearrange("p (a b) -> p a b", a=4))))(),
                            r=["pbT"], w=["a_T"])
                    else:
                        add("vector", (lambda g4=g4: (lambda v: v.tensor_copy(
                            bass.AP(a_T.tensor, 512 * g4, [[a_T.ap[0][0], 128], [128, 4], [1, 128]]),
                            pbU[:, 0:512].rearrange("p (a b) -> p a b", a=4))))(),
                            r=["pbU"], w=["a_T"])
                if qb != 0:
                    add("sync", (lambda qb=qb: (lambda s: s.dma_start(
                        out=dascr[128 * qb:128 * (qb + 1), :], in_=a_T[:])))(),
                        r=["a_T"], w=[f"ascr:{qb}"], dma=True)
            return tail
        pending_tail = make_tail()

    gu_preload(dWg[0], 2, "g0pre")
    gu_preload(dWg[0], 3, "g0pre")
    for qb in list(range(1, NT)):
        sg = stg[qb % 2]
        gk = f"stg{qb % 2}"
        if qb > 2:
            add("sync", (lambda qb=qb, sg=sg: (lambda s: s.dma_start(
                out=sg[:], in_=dascr[128 * qb:128 * (qb + 1), :])))(),
                r=[f"ascr:{qb}"], w=[gk], dma=True)
        for dh in range(2):
            bank = [5, 4][dh]
            pkey = f"pb{bank}"
            for ti in range(8):
                add("tensor", (lambda ti=ti, dh=dh, bank=bank, sg=sg: (lambda t: t.matmul(
                    pb[bank][:, 0:512], sg[:, 128 * ti:128 * (ti + 1)],
                    ovl[:, WT_OFF + 1024 * ti + 512 * dh:WT_OFF + 1024 * ti + 512 * (dh + 1)],
                    start=(ti == 0), stop=(ti == 7))))(),
                    r=[gk] + [wtI(4 * dh + j) for j in range(4)], w=[pkey])
            add("vector", (lambda dh=dh, bank=bank: (lambda v: v.tensor_tensor(
                sto[dh][:], pb[bank][:, 0:512],
                xnb[:, 512 * dh:512 * (dh + 1)], ALU.add)))(),
                r=[pkey, "xnb"], w=[f"sto{dh}"])
            add("gpsimd", (lambda qb=qb, dh=dh: (lambda g: g.dma_start(
                out=dpscr[128 * qb:128 * (qb + 1), 512 * dh:512 * (dh + 1)],
                in_=sto[dh][:])))(), r=[f"sto{dh}"], w=[f"pscr:{qb}"], dma=True)
        if qb == 14:
            gu_preload(dWg[0], 0, "g0pre")
        if qb == 15:
            gu_preload(dWg[0], 1, "g0pre")

    pending_tail()
    pending_tail = None
    for dh in range(2):
        bank = [5, 4][dh]
        pkey = f"pb{bank}"
        for ti in range(8):
            add("tensor", (lambda ti=ti, dh=dh, bank=bank: (lambda t: t.matmul(
                pb[bank][:, 0:512], a_T[:, 128 * ti:128 * (ti + 1)],
                ovl[:, WT_OFF + 1024 * ti + 512 * dh:WT_OFF + 1024 * ti + 512 * (dh + 1)],
                start=(ti == 0), stop=(ti == 7))))(),
                r=["a_T"] + [wtI(4 * dh + j) for j in range(4)], w=[pkey])
        add("vector", (lambda dh=dh, bank=bank: (lambda v: v.tensor_tensor(
            sto[dh][:], pb[bank][:, 0:512],
            xnb[:, 512 * dh:512 * (dh + 1)], ALU.add)))(),
            r=[pkey, "xnb"], w=[f"sto{dh}"])
        add("gpsimd", (lambda dh=dh: (lambda g: g.dma_start(
            out=dpscr[0:128, 512 * dh:512 * (dh + 1)],
            in_=sto[dh][:])))(), r=[f"sto{dh}"], w=["pscr:0"], dma=True)


    def gu_G(dWg_, tag, skip_loads=0, inject=None, forder=None):
        for fi, f_ in enumerate(forder if forder is not None else range(16)):
            if inject and fi in inject:
                inject[fi]()
            sg = stg[f_ % 4]
            gk = f"stg{f_ % 4}"
            if fi >= skip_loads:
                gu_preload(dWg_, f_, tag)
            for ncs in range(NC512):
                bank = ncs % 2
                pkey = f"pb{bank}"
                for td in range(8):
                    add("tensor", (lambda f_=f_, ncs=ncs, td=td, bank=bank, sg=sg: (lambda t: t.matmul(
                        pb[bank][:, 0:512], sg[:, 128 * td:128 * (td + 1)],
                        xnT[:, T * td + 512 * ncs:T * td + 512 * (ncs + 1)],
                        start=(td == 0), stop=(td == 7))))(), r=[gk, "xnT"], w=[pkey])
                add("scalar", (lambda f_=f_, ncs=ncs, bank=bank: (lambda s: s.activation(
                    big[:, H_OFF + T * f_ + 512 * ncs:H_OFF + T * f_ + 512 * (ncs + 1)],
                    pb[bank][:, 0:512], AF.Silu)))(), r=[pkey], w=list(h_k(f_)))
    def gu_U(dWu_, e_idx, tag, inject=None):
        for f_ in range(16):
            if inject and f_ in inject:
                inject[f_]()
            sg = stg[f_ % 4]
            gk = f"stg{f_ % 4}"
            gu_preload(dWu_, f_, tag)
            for ncs in range(NC512):
                bank = 2 + ncs % 2
                pkey = f"pb{bank}"
                for td in range(8):
                    add("tensor", (lambda f_=f_, ncs=ncs, td=td, bank=bank, sg=sg: (lambda t: t.matmul(
                        pb[bank][:, 0:512], sg[:, 128 * td:128 * (td + 1)],
                        xnT[:, T * td + 512 * ncs:T * td + 512 * (ncs + 1)],
                        start=(td == 0), stop=(td == 7))))(), r=[gk, "xnT"], w=[pkey])
                add("vector", (lambda f_=f_, ncs=ncs, bank=bank: (lambda v: v.tensor_mul(
                    big[:, H_OFF + T * f_ + 512 * ncs:H_OFF + T * f_ + 512 * (ncs + 1)],
                    big[:, H_OFF + T * f_ + 512 * ncs:H_OFF + T * f_ + 512 * (ncs + 1)],
                    pb[bank][:, 0:512])))(), r=[pkey] + list(h_k(f_)), w=list(h_k(f_)))
                if e_idx is not None:
                    add("vector", (lambda f_=f_, ncs=ncs, e_idx=e_idx: (lambda v: v.tensor_mul(
                        big[:, H_OFF + T * f_ + 512 * ncs:H_OFF + T * f_ + 512 * (ncs + 1)],
                        big[:, H_OFF + T * f_ + 512 * ncs:H_OFF + T * f_ + 512 * (ncs + 1)],
                        wbc[e_idx][:, 512 * ncs:512 * (ncs + 1)])))(),
                        r=[f"wbc{e_idx}"] + list(h_k(f_)), w=list(h_k(f_)))

    def wdc_load(dWd_, dh, buf, c, tag):
        off = (WDC_OFF if buf == 0 else WDC2_OFF) + 2048 * c
        key = f"big:{off // 2048}"
        add("gpsimd", (lambda dWd_=dWd_, dh=dh, off=off, c=c: (lambda g: g.dma_start(
            out=bass.AP(big.tensor, off, [[PSTEP_B, 128], [512, 4], [1, 512]]),
            in_=dWd_.rearrange("(t p) d -> p t d", p=128)[:, 4 * c:4 * (c + 1), 512 * dh:512 * (dh + 1)])))(),
            w=[key], dma=True, tag=tag)

    def d2_loads(ntile, dh):
        sl = ntile % 4
        sxl = ntile % 2
        sx = stxr[:, 512 * sxl:512 * (sxl + 1)]
        add("scalar", (lambda ntile=ntile, dh=dh, sx=sx: (lambda s: s.dma_start(
            out=sx,
            in_=dx[128 * ntile:128 * (ntile + 1), 512 * dh:512 * (dh + 1)])))(),
            w=[f"sx{sxl}"], dma=True)
        pp = stg[sl][:, 0:512]
        add("scalar", (lambda ntile=ntile, dh=dh, pp=pp: (lambda s: s.dma_start(
            out=pp,
            in_=dpscr[128 * ntile:128 * (ntile + 1), 512 * dh:512 * (dh + 1)])))(),
            r=[f"pscr:{ntile}"], w=[f"stg{sl}"], dma=True)

    def d_stage_half(mode, dh, buf, tag):
        woff = WDC_OFF if buf == 0 else WDC2_OFF
        if mode == 2:
            for nt_ in range(2):
                d2_loads(nt_, dh)
        for g8 in range(T // 256):
            grp = g8 % 3
            banks = (2 * grp, 2 * grp + 1)
            for f_ in range(16):
                for sub in range(2):
                    ntile = 2 * g8 + sub
                    bkv = banks[sub]
                    add("tensor", (lambda f_=f_, ntile=ntile, bk=bkv, woff=woff: (lambda t: t.matmul(
                        pb[bk][:, 0:512],
                        big[:, H_OFF + T * f_ + 128 * ntile:H_OFF + T * f_ + 128 * (ntile + 1)],
                        big[:, woff + 512 * f_:woff + 512 * (f_ + 1)],
                        start=(f_ == 0), stop=(f_ == 15))))(),
                        r=list(h_k(f_)) + [f"big:{(woff + 512 * f_) // 2048}"], w=[f"pb{bkv}"])
            for sub in range(2):
                ntile = 2 * g8 + sub
                so = sto[ntile % 2]
                sok = f"sto{ntile % 2}"
                pkey = f"pb{banks[sub]}"
                if mode == 2:
                    sl = ntile % 4
                    sxl = ntile % 2
                    sx = stxr[:, 512 * sxl:512 * (sxl + 1)]
                    pp = stg[sl][:, 0:512]
                    add("vector", (lambda bk=banks[sub], so=so, sx=sx: (lambda v: v.tensor_add(
                        so[:], pb[bk][:, 0:512], sx)))(), r=[pkey, f"sx{sxl}"], w=[sok])
                    add("vector", (lambda so=so, pp=pp: (lambda v: v.tensor_add(
                        so[:], so[:], pp)))(), r=[sok, f"stg{sl}"], w=[sok])
                else:
                    add("vector", (lambda bk=banks[sub], so=so: (lambda v: v.tensor_copy(
                        so[:], pb[bk][:, 0:512])))(), r=[pkey], w=[sok])
                okey = f"out:{ntile}:{dh}"
                if mode == 0:
                    add("gpsimd", (lambda so=so, ntile=ntile, dh=dh: (lambda g: g.dma_start(
                        out=dout[128 * ntile:128 * (ntile + 1), 512 * dh:512 * (dh + 1)],
                        in_=so[:])))(), r=[sok, okey], w=[okey], dma=True, tag=tag)
                else:
                    add("gpsimd", (lambda so=so, ntile=ntile, dh=dh: (lambda g: g.dma_start(
                        out=dout[128 * ntile:128 * (ntile + 1), 512 * dh:512 * (dh + 1)],
                        in_=so[:], accum_op=ALU.add)))(), r=[sok, okey], w=[okey], dma=True, tag=tag)
            if mode == 2:
                for nt_ in (2 * g8 + 2, 2 * g8 + 3):
                    if nt_ < 16:
                        d2_loads(nt_, dh)

    def wdc_injects(dWd_, tag):
        inj = {}
        for c in range(4):
            inj[1 + 2 * c] = (lambda dWd_=dWd_, c=c: wdc_load(dWd_, 0, 0, c, tag + "a"))
            inj[8 + 2 * c] = (lambda dWd_=dWd_, c=c: wdc_load(dWd_, 1, 1, c, tag + "b"))
        return inj

    gu_G(dWg[0], "g0", skip_loads=4, forder=[2, 3, 0, 1] + list(range(4, 16)))
    gu_U(dWu[0], 0, "u0", inject=wdc_injects(dWd[0], "wd0"))
    for _pf in range(4):
        gu_preload(dWg[1], _pf, "pre1")
    d_stage_half(0, 0, 0, "d0h0")
    d_stage_half(0, 1, 1, "d0h1")
    gu_G(dWg[1], "g1", skip_loads=4)
    gu_U(dWu[1], 1, "u1", inject=wdc_injects(dWd[1], "wd1"))
    for _pf in range(4):
        gu_preload(dSg, _pf, "pres")
    d_stage_half(1, 0, 0, "d1h0")
    d_stage_half(1, 1, 1, "d1h1")
    gu_G(dSg, "gs", skip_loads=4)
    gu_U(dSu, None, "us", inject=wdc_injects(dSd, "sd"))
    d_stage_half(2, 0, 0, "dsh0")
    d_stage_half(2, 1, 1, "dsh1")

    B.finalize(nc)
    fc = B.final_counts
    B.emit(nc)
    with nc.Block() as blk:
        @blk.gpsimd
        def _(g):
            for e in ("gpsimd", "sync", "scalar"):
                n = B.qcnt[e]
                for sl in range(B.R):
                    full = 16 * ((n - sl + B.R - 1) // B.R) if n > 0 else 0
                    if full > 0:
                        g.wait_ge(B.sems[("q", e, sl)], full)
    return nc


def kernel(L0_attn_x, L0_attn_Wq, L0_attn_bq, L0_attn_Wk, L0_attn_bk, L0_attn_Wv,
           L0_attn_bv, L0_attn_Wo, L0_attn_bo, L0_ffn_W_router, L0_ffn_router_bias,
           L0_ffn_Wg, L0_ffn_Wu, L0_ffn_Wd, L0_ffn_Sg, L0_ffn_Su, L0_ffn_Sd):
    T = 2048
    x = np.ascontiguousarray(np.asarray(L0_attn_x, dtype=np.float32))
    Bsz = x.shape[0]
    mA, mB, mC = _masks()
    com = {
        "Wq": np.ascontiguousarray(L0_attn_Wq, dtype=np.float32),
        "bq": np.ascontiguousarray(np.asarray(L0_attn_bq, dtype=np.float32).reshape(1, D)),
        "Wk": np.ascontiguousarray(L0_attn_Wk, dtype=np.float32),
        "bk": np.ascontiguousarray(np.asarray(L0_attn_bk, dtype=np.float32).reshape(1, D)),
        "Wv": np.ascontiguousarray(L0_attn_Wv, dtype=np.float32),
        "bv": np.ascontiguousarray(np.asarray(L0_attn_bv, dtype=np.float32).reshape(1, D)),
        "Wo": np.ascontiguousarray(L0_attn_Wo, dtype=np.float32),
        "bo": np.ascontiguousarray(np.asarray(L0_attn_bo, dtype=np.float32).reshape(1, D)),
        "Wr": np.ascontiguousarray(L0_ffn_W_router, dtype=np.float32),
        "rb": np.ascontiguousarray(np.asarray(L0_ffn_router_bias, dtype=np.float32).reshape(1, NE)),
        "Wg0": np.ascontiguousarray(np.asarray(L0_ffn_Wg, dtype=np.float32)[0]),
        "Wg1": np.ascontiguousarray(np.asarray(L0_ffn_Wg, dtype=np.float32)[1]),
        "Wu0": np.ascontiguousarray(np.asarray(L0_ffn_Wu, dtype=np.float32)[0]),
        "Wu1": np.ascontiguousarray(np.asarray(L0_ffn_Wu, dtype=np.float32)[1]),
        "Wd0": np.ascontiguousarray(np.asarray(L0_ffn_Wd, dtype=np.float32)[0]),
        "Wd1": np.ascontiguousarray(np.asarray(L0_ffn_Wd, dtype=np.float32)[1]),
        "Sg": np.ascontiguousarray(L0_ffn_Sg, dtype=np.float32),
        "Su": np.ascontiguousarray(L0_ffn_Su, dtype=np.float32),
        "Sd": np.ascontiguousarray(L0_ffn_Sd, dtype=np.float32),
        "mA": mA, "mB": mB, "mC": mC,
        "ident": np.eye(128, dtype=np.float32),
    }
    nc = build_program(T)
    in_maps = []
    for b in range(Bsz):
        m = dict(com)
        m["x"] = np.ascontiguousarray(x[b])
        in_maps.append(m)
    res = run_bass_kernel_spmd(nc, in_maps, core_ids=list(range(Bsz)))
    out = np.stack([np.asarray(r["out"], dtype=np.float32) for r in res.results], axis=0)
    return out


# version 33
# speedup vs baseline: 1.0063x; 1.0063x over previous
import numpy as np
import concourse.bass as bass
import concourse.mybir as mybir
from concourse.bass_utils import run_bass_kernel_spmd

F32 = mybir.dt.float32
BF16 = mybir.dt.bfloat16
AF = mybir.ActivationFunctionType
ALU = mybir.AluOpType
AX = mybir.AxisListType

D = 1024
NH = 16
HD = 64
F = 2048
NE = 2
EPS = 1e-5

ENGINES = ("tensor", "vector", "scalar", "gpsimd", "sync")


class _Op:
    __slots__ = ("eng", "fn", "reads", "writes", "is_dma", "inc", "sem_val",
                 "deps", "waits", "snap", "tag", "slot", "gen")

    def __init__(self, eng, fn, reads, writes, is_dma, tag):
        self.eng = eng
        self.fn = fn
        self.reads = tuple(reads)
        self.writes = tuple(writes)
        self.is_dma = is_dma
        self.inc = is_dma
        self.sem_val = None
        self.deps = set()
        self.waits = []
        self.snap = None
        self.tag = tag


class Builder:
    def __init__(self):
        self.ops = {e: [] for e in ENGINES}
        self.glob = []
        self.state = {}

    def add(self, eng, fn, r=(), w=(), dma=False, tag=""):
        rpb = tuple(k for k in r if k.startswith("pb"))
        if rpb:
            r = tuple(k for k in r if not k.startswith("pb"))
            w = tuple(w) + rpb
        op = _Op(eng, fn, r, w, dma, tag)
        idx = len(self.ops[eng])
        deps = set()
        for k in op.reads:
            st = self.state.get(k)
            if st and st["w"] is not None:
                deps.add(st["w"])
        for k in op.writes:
            st = self.state.get(k)
            if st:
                if st["w"] is not None:
                    deps.add(st["w"])
                deps.update(st["r"])
        op.deps = {d for d in deps if not (d[0] == eng == "tensor" and not dma
                                           and not self.ops[eng][d[1]].is_dma)}
        self.ops[eng].append(op)
        self.glob.append(op)
        ref = (eng, idx)
        for k in op.reads:
            self.state.setdefault(k, {"w": None, "r": []})["r"].append(ref)
        for k in op.writes:
            self.state[k] = {"w": ref, "r": []}
        return op

    def finalize(self, nc):
        R = 8
        self.R = R
        for op in self.glob:
            for (e, j) in op.deps:
                self.ops[e][j].inc = True
        sems = {}
        for e in ENGINES:
            sems[("c", e)] = nc.semaphore(f"sc_{e}").__enter__()
        for e in ("gpsimd", "sync", "scalar"):
            for sl in range(R):
                sems[("q", e, sl)] = nc.semaphore(f"q_{e}_{sl}").__enter__()
        cum = {e: 0 for e in ENGINES}
        qcnt = {e: 0 for e in ENGINES}
        for e in ENGINES:
            for op in self.ops[e]:
                if op.is_dma:
                    i = qcnt[e]
                    qcnt[e] = i + 1
                    op.slot = (i % R)
                    op.gen = (i // R)
                    op.sem_val = 16 * (op.gen + 1)
                elif op.inc:
                    cum[e] += 1
                    op.sem_val = cum[e]
        self.final_counts = {("c", e): cum[e] for e in ENGINES}
        self.qcnt = qcnt
        wm = {e: {} for e in ENGINES}

        def semkey(op):
            if op.is_dma:
                return ("q", op.eng, op.slot)
            return ("c", op.eng)

        for op in self.glob:
            e = op.eng
            needed = []
            for (e2, j2) in op.deps:
                p = self.ops[e2][j2]
                k = semkey(p)
                if wm[e].get(k, 0) >= p.sem_val:
                    continue
                needed.append((k, p.sem_val, p))
            if op.is_dma and op.gen > 0:
                k = ("q", e, op.slot)
                if wm[e].get(k, 0) < 16 * op.gen:
                    needed.append((k, 16 * op.gen, None))
            for (k, v, p) in sorted(needed, key=lambda t: (t[0], t[1])):
                if wm[e].get(k, 0) >= v:
                    continue
                op.waits.append((k, v))
                wm[e][k] = v
                if p is not None and p.snap:
                    for k2, v2 in p.snap.items():
                        if wm[e].get(k2, 0) < v2:
                            wm[e][k2] = v2
            if op.inc:
                snap = dict(wm[e])
                k = semkey(op)
                snap[k] = max(snap.get(k, 0), op.sem_val)
                op.snap = snap
        self.sems = sems

    def emit(self, nc):
        sems = self.sems
        with nc.Block() as block:
            for e in ENGINES:
                ops = self.ops[e]

                def make(eng_ops, ename):
                    def run(eng):
                        for op in eng_ops:
                            for (k, v) in op.waits:
                                eng.wait_ge(sems[k], v)
                            inst = op.fn(eng)
                            if op.is_dma:
                                inst.then_inc(sems[("q", ename, op.slot)], 16)
                            elif op.inc:
                                inst.then_inc(sems[("c", ename)], 1)
                    return run

                getattr(block, e)(make(ops, e))


def _masks():
    p = np.arange(128)[:, None]
    j = np.arange(128)[None, :]
    low = (p <= j).astype(np.float32)
    up = (p >= j).astype(np.float32)
    col0 = np.broadcast_to((p == 0), (128, 128)).astype(np.float32)
    one = np.ones((128, 1), np.float32)
    mA = np.ascontiguousarray(np.concatenate([np.maximum(low, col0), one], axis=1))
    mB = np.ascontiguousarray(np.maximum(up, col0))
    mC = np.ascontiguousarray(np.concatenate([col0, up, low, one], axis=1))
    return mA, mB, mC


def build_program(T=2048):
    NT = T // 128
    NC512 = T // 512
    nc = bass.Bass("TRN2", target_bir_lowering=False)
    B = Builder()
    add = B.add

    dx = nc.dram_tensor("x", [T, D], F32, kind="ExternalInput").ap()
    dWq = nc.dram_tensor("Wq", [D, D], F32, kind="ExternalInput").ap()
    dbq = nc.dram_tensor("bq", [1, D], F32, kind="ExternalInput").ap()
    dWk = nc.dram_tensor("Wk", [D, D], F32, kind="ExternalInput").ap()
    dbk = nc.dram_tensor("bk", [1, D], F32, kind="ExternalInput").ap()
    dWv = nc.dram_tensor("Wv", [D, D], F32, kind="ExternalInput").ap()
    dbv = nc.dram_tensor("bv", [1, D], F32, kind="ExternalInput").ap()
    dWo = nc.dram_tensor("Wo", [D, D], F32, kind="ExternalInput").ap()
    dbo = nc.dram_tensor("bo", [1, D], F32, kind="ExternalInput").ap()
    dWr = nc.dram_tensor("Wr", [D, NE], F32, kind="ExternalInput").ap()
    drb = nc.dram_tensor("rb", [1, NE], F32, kind="ExternalInput").ap()
    dWg = [nc.dram_tensor(f"Wg{e}", [D, F], F32, kind="ExternalInput").ap() for e in range(NE)]
    dWu = [nc.dram_tensor(f"Wu{e}", [D, F], F32, kind="ExternalInput").ap() for e in range(NE)]
    dWd = [nc.dram_tensor(f"Wd{e}", [F, D], F32, kind="ExternalInput").ap() for e in range(NE)]
    dSg = nc.dram_tensor("Sg", [D, F], F32, kind="ExternalInput").ap()
    dSu = nc.dram_tensor("Su", [D, F], F32, kind="ExternalInput").ap()
    dSd = nc.dram_tensor("Sd", [F, D], F32, kind="ExternalInput").ap()
    dmA = nc.dram_tensor("mA", [128, 129], F32, kind="ExternalInput").ap()
    dmB = nc.dram_tensor("mB", [128, 128], F32, kind="ExternalInput").ap()
    dmC = nc.dram_tensor("mC", [128, 385], F32, kind="ExternalInput").ap()
    dident = nc.dram_tensor("ident", [128, 128], F32, kind="ExternalInput").ap()
    dout = nc.dram_tensor("out", [T, D], F32, kind="ExternalOutput").ap()
    dpscr = nc.dram_tensor("pscr", [T, D], BF16, kind="Internal").ap()
    dascr = nc.dram_tensor("ascr", [T, D], BF16, kind="Internal").ap()

    S = lambda n, sh, dt: nc.alloc_sbuf_tensor(n, sh, dt).ap()
    xnT = S("xnT", [128, 8 * T], BF16)
    BIGC = max(16 * T + NT * 66 * NH, 16 * F + 2 * 16 * 512)
    big = S("big", [128, BIGC], BF16)
    QT_OFF, KT_OFF, V_OFF = 0, 8 * T, 16 * T
    H_OFF, WDC_OFF, WDC2_OFF = 0, 16 * F, 16 * F + 8192
    ovl = S("ovl", [128, 12288], BF16)
    WN_OFF, WT_OFF = 0, 4096
    stx = [S("stx0", [128, 1024], F32)]
    stg = [S(f"stg{i}", [128, 1024], BF16) for i in range(4)]
    _sto = S("sto", [128, 1024], F32)
    sto = [_sto[:, 0:512], _sto[:, 512:1024]]
    stxr = S("stxr", [128, 1024], F32)
    wbc = [S(f"wbc{e}", [128, T], BF16) for e in range(NE)]
    exT = [S(f"exT{i}", [128, 385], BF16) for i in range(6)]
    a_nat = S("a_nat", [128, 1024], BF16)
    a_T = S("a_T", [128, 1024], BF16)
    mskA = S("mskA", [128, 129], BF16)
    mskB = S("mskB", [128, 128], BF16)
    mskC = S("mskC", [128, 385], BF16)
    ident = S("ident_s", [128, 128], BF16)
    bias_qkv = S("bias_qkv", [1, D], BF16)
    bias_qkv2 = S("bias_qkv2", [1, D], BF16)
    bias_o = S("bias_o", [1, D], BF16)
    bqkT = S("bqkT", [128, 16], F32)
    r0bf = S("r0bf", [128, 448], BF16)
    ones = S("ones_s", [1, 512], BF16)
    stat = S("stat", [128, 40], F32)
    kons = S("kons", [128, 1], F32)
    xnb = S("xnb", [128, D], BF16)
    wrc = S("wrc", [128, 16], F32)
    wrdf = S("wrdf", [128, 8], F32)
    wrdH = S("wrdH", [128, 8], BF16)
    wrdL = S("wrdL", [128, 8], BF16)
    wrow = S("wrow", [1, 1024], BF16)
    dhi = S("dhi", [128, 16], F32)
    thrc = S("thrc", [128, 1], F32)
    rbs = S("rbs", [1, 2], F32)
    rbt = S("rbt", [1, 2], F32)
    rbtb = S("rbtb", [1, 2], BF16)
    rtr = S("rtr", [128, 16 * 5], F32)
    w0c = S("w0c", [128, 16], F32)
    w1c = S("w1c", [128, 16], F32)
    wcb = S("wcb", [128, 16], BF16)
    rsb = S("rsb", [128, 16], F32)
    r0tmp = S("r0tmp", [128, 468], F32)
    wrdbc = wbc[0][:, 0:1024] if T >= 1024 else S("wrdbc", [128, 1024], BF16)[:]
    stp = stg[0][:, 0:512]

    P = lambda n, sh, dt: nc.alloc_psum_tensor(n, sh, dt).ap()
    pb = [P(f"pb{i}", [128, 512], F32) for i in range(6)]
    pbU = P("pbU", [128, 1024], BF16)
    pbT = P("pbT", [128, 1024], BF16)
    PSTEP_B = big.ap[0][0]
    PSTEP_T = pbT.ap[0][0]

    def big_keys(lo_col, hi_col):
        return tuple(f"big:{c}" for c in range(lo_col // 2048, (hi_col + 2047) // 2048))

    qt_k = lambda it: big_keys(QT_OFF + T * it, QT_OFF + T * (it + 1))
    kt_k = lambda it: big_keys(KT_OFF + T * it, KT_OFF + T * (it + 1))
    v_k = lambda ntl: big_keys(V_OFF + 66 * NH * ntl, V_OFF + 66 * NH * (ntl + 1))
    h_k = lambda f: big_keys(H_OFF + T * f, H_OFF + T * (f + 1))

    wn_k = ("ovl:wn",)
    wtI = lambda ti: f"wtI{ti}"

    add("gpsimd", lambda g: g.dma_start(out=ident[:], in_=dident), w=["ident"], dma=True)
    add("gpsimd", lambda g: g.dma_start(out=mskA[:], in_=dmA), w=["mskA"], dma=True)
    add("gpsimd", lambda g: g.dma_start(out=mskB[:], in_=dmB), w=["mskB"], dma=True)
    add("gpsimd", lambda g: g.dma_start(out=mskC[:], in_=dmC), w=["mskC"], dma=True)
    add("vector", lambda v: v.memset(ones[:], 1.0), w=["ones"])
    add("vector", lambda v: v.memset(kons[:, 0:1], EPS), w=["kons"])
    vones = bass.AP(big.tensor, V_OFF + 64, [[PSTEP_B, 128], [66, NH * NT], [1, 1]])
    add("vector", lambda v: v.memset(vones, 1.0),
        w=list(big_keys(V_OFF, V_OFF + 66 * NH * NT)))

    add("sync", lambda s: s.dma_start(out=rbs[:], in_=drb), w=["rbs"], dma=True)
    add("vector", lambda v: v.tensor_sub(rbt[0:1, 0:1], rbs[0:1, 1:2], rbs[0:1, 0:1]), r=["rbs"], w=["rbt0"])
    add("vector", lambda v: v.tensor_copy(rbtb[0:1, 0:1], rbt[0:1, 0:1]), r=["rbt0"], w=["rbtb0"])
    add("vector", lambda v: v.tensor_sub(rbt[0:1, 1:2], rbt[0:1, 0:1], rbtb[0:1, 0:1]), r=["rbt0", "rbtb0"], w=["rbt1"])
    add("vector", lambda v: v.tensor_copy(rbtb[0:1, 1:2], rbt[0:1, 1:2]), r=["rbt1"], w=["rbtb1"])
    add("sync", lambda s: s.dma_start(out=wrc.rearrange("p (t e) -> p t e", t=8),
                                      in_=dWr.rearrange("(t p) e -> p t e", p=128)), w=["wrc"], dma=True)
    add("vector", lambda v: v.tensor_sub(wrdf[:], wrc[:, 0:16:2], wrc[:, 1:16:2]), r=["wrc"], w=["wrdf"])
    add("vector", lambda v: v.tensor_copy(wrdH[:], wrdf[:]), r=["wrdf"], w=["wrdH"])
    add("vector", lambda v: v.tensor_sub(wrdL[:], wrdf[:], wrdH[:]), r=["wrdf", "wrdH"], w=["wrdL"])
    add("tensor", lambda t: t.matmul(pb[1][:, 32:33], ones[0:1, 0:128], rbtb[0:1, 0:1], start=True, stop=False),
        r=["rbtb0", "ones"], w=["pb1"])
    add("tensor", lambda t: t.matmul(pb[1][:, 32:33], ones[0:1, 0:128], rbtb[0:1, 1:2], start=False, stop=True),
        r=["rbtb1", "ones"], w=["pb1"])
    add("vector", lambda v: v.tensor_copy(thrc[:], pb[1][:, 32:33]), r=["pb1"], w=["thrc"])
    for t_ in range(8):
        add("tensor", (lambda t_=t_: (lambda tt: tt.transpose(
            pbT[0:1, 0:128], wrdH[:, t_:t_ + 1], ident[:])))(), r=["wrdH", "ident"], w=["pbT"])
        add("vector", (lambda t_=t_: (lambda v: v.tensor_copy(
            wrow[0:1, 128 * t_:128 * (t_ + 1)], pbT[0:1, 0:128])))(), r=["pbT"], w=["wrow"])
    for c in range(2):
        add("tensor", (lambda c=c: (lambda t: t.matmul(
            pb[1][:, 0:512], ones[0:1, 0:128], wrow[0:1, 512 * c:512 * (c + 1)],
            start=True, stop=True)))(), r=["wrow", "ones"], w=["pb1"])
        add("vector", (lambda c=c: (lambda v: v.tensor_copy(
            wrdbc[:, 512 * c:512 * (c + 1)], pb[1][:, 0:512])))(), r=["pb1"],
            w=["wbc0" if T >= 1024 else "wrdbc"])

    def load_w_quarter(dW, q, tag):
        qb_ = 2048 * (q % 2)
        add("gpsimd", (lambda dW=dW, q=q, qb_=qb_: (lambda g: g.dma_start(
            out=ovl[:, qb_:qb_ + 2048].rearrange("p (a b) -> p a b", a=2),
            in_=dW.rearrange("(t p) d -> p t d", p=128)[:, 2 * q:2 * (q + 1), :])))(),
            w=[f"wn{q % 2}"], dma=True, tag=tag)

    def transpose_w_quarter(q, tag):
        opst = ovl.ap[0][0]
        qb_ = 2048 * (q % 2)
        for t_il in range(2):
            t_i = 2 * q + t_il
            for g4 in range(2):
                par = (t_il * 2 + g4) % 2
                dst = pbT if par == 0 else pbU
                dkey = "pbT" if par == 0 else "pbU"
                for t_d4 in range(4):
                    t_d = 4 * g4 + t_d4
                    add("tensor", (lambda t_il=t_il, t_d=t_d, t_d4=t_d4, qb_=qb_, dst=dst: (lambda t: t.transpose(
                        dst[:, 128 * t_d4:128 * (t_d4 + 1)],
                        ovl[:, qb_ + 1024 * t_il + 128 * t_d:qb_ + 1024 * t_il + 128 * (t_d + 1)],
                        ident[:])))(), r=[f"wn{q % 2}", "ident"], w=[dkey])
                if par == 0:
                    add("scalar", (lambda t_i=t_i, g4=g4, opst=opst: (lambda s: s.copy(
                        bass.AP(ovl.tensor, WT_OFF + 1024 * 4 * g4 + 128 * t_i,
                                [[opst, 128], [1024, 4], [1, 128]]),
                        pbT[:, 0:512].rearrange("p (a b) -> p a b", a=4))))(),
                        r=["pbT"], w=[wtI(t_i)])
                else:
                    add("vector", (lambda t_i=t_i, g4=g4, opst=opst: (lambda v: v.tensor_copy(
                        bass.AP(ovl.tensor, WT_OFF + 1024 * 4 * g4 + 128 * t_i,
                                [[opst, 128], [1024, 4], [1, 128]]),
                        pbU[:, 0:512].rearrange("p (a b) -> p a b", a=4))))(),
                        r=["pbU"], w=[wtI(t_i)])

    def gen_qk_one(dstoff, dst_keys_fn, bcol, nck, it, bank, eng):
        pkey = f"pb{bank}"
        for td in range(8):
            add("tensor", (lambda it=it, nck=nck, td=td, bank=bank: (lambda t: t.matmul(
                pb[bank][:, 0:512],
                ovl[:, WT_OFF + 1024 * td + 128 * it: WT_OFF + 1024 * td + 128 * (it + 1)],
                xnT[:, T * td + 512 * nck: T * td + 512 * (nck + 1)],
                start=(td == 0), stop=(td == 7))))(), r=[wtI(it), "xnT"], w=[pkey])
        if eng == "scalar":
            add("scalar", (lambda it=it, nck=nck, bank=bank, dstoff=dstoff, bcol=bcol: (lambda s: s.activation(
                bass.AP(big.tensor, dstoff + T * it + 512 * nck, [[PSTEP_B, 128], [1, 512]]),
                pb[bank][:, 0:512], AF.Identity,
                bias=bqkT[:, bcol + it:bcol + it + 1], scale=1.0)))(),
                r=[pkey, "bqkT"], w=list(dst_keys_fn(it)))
        else:
            add("vector", (lambda it=it, nck=nck, bank=bank, dstoff=dstoff, bcol=bcol: (lambda v: v.tensor_tensor(
                bass.AP(big.tensor, dstoff + T * it + 512 * nck, [[PSTEP_B, 128], [1, 512]]),
                pb[bank][:, 0:512],
                bass.AP(bqkT.tensor, bcol + it, [[bqkT.ap[0][0], 128], [0, 512]]),
                ALU.add)))(), r=[pkey, "bqkT"], w=list(dst_keys_fn(it)))

    def gen_qkt_nck(dstoff, dst_keys_fn, bcol, nck):
        for it in range(8):
            gen_qk_one(dstoff, dst_keys_fn, bcol, nck, it,
                       [5, 4, 3, 2][it % 4], "scalar" if it % 2 == 0 else "vector")

    def gen_kt_itpair(it0):
        for it in (it0, it0 + 1):
            for nck in range(NC512):
                gen_qk_one(KT_OFF, kt_k, 8, nck, it,
                           [5, 4, 3, 2][nck % 4], "scalar" if nck % 2 == 0 else "vector")

    def vgen_one(ntl, ih):
        wkeys = [wtI(4 * ih + j) for j in range(4)]
        bank = [5, 4, 3, 2][(ntl + 16 * ih) % 4]
        pkey = f"pb{bank}"
        for td in range(8):
            add("tensor", (lambda ntl=ntl, ih=ih, td=td, bank=bank: (lambda t: t.matmul(
                pb[bank][:, 0:512],
                xnT[:, T * td + 128 * ntl: T * td + 128 * (ntl + 1)],
                ovl[:, WT_OFF + 1024 * td + 512 * ih: WT_OFF + 1024 * td + 512 * (ih + 1)],
                start=(td == 0), stop=(td == 7))))(), r=wkeys + ["xnT"], w=[pkey])
        add("vector", (lambda ntl=ntl, ih=ih, bank=bank: (lambda v: v.tensor_tensor(
            bass.AP(big.tensor, V_OFF + 66 * NH * ntl + 66 * 8 * ih,
                    [[PSTEP_B, 128], [66, 8], [1, 64]]),
            pb[bank].rearrange("p (h d) -> p h d", h=8),
            bass.AP(a_nat.tensor, 512 * ih, [[a_nat.ap[0][0], 128], [64, 8], [1, 64]]),
            ALU.add)))(), r=[pkey, "a_nat"], w=list(v_k(ntl)))

    def load_transpose_w(dW, tag):
        load_w_quarter(dW, 0, tag)
        load_w_quarter(dW, 1, tag)
        transpose_w_quarter(0, tag)
        load_w_quarter(dW, 2, tag)
        transpose_w_quarter(1, tag)
        load_w_quarter(dW, 3, tag)
        transpose_w_quarter(2, tag)
        transpose_w_quarter(3, tag)

    add("gpsimd", lambda g: g.dma_start(out=bias_qkv[:], in_=dbq), w=["bias_qkv"], dma=True)
    load_transpose_w(dWq, "Wq")
    add("gpsimd", lambda g: g.dma_start(out=bias_qkv2[:], in_=dbk), w=["bias_qkv2"], dma=True)
    for bi_, (brow, brk) in enumerate(((bias_qkv, "bias_qkv"), (bias_qkv2, "bias_qkv2"))):
        for it in range(8):
            add("tensor", (lambda bi_=bi_, it=it, brow=brow, brk=brk: (lambda t: t.matmul(
                pb[1][:, 8 * bi_ + it:8 * bi_ + it + 1], brow[0:1, 128 * it:128 * (it + 1)],
                ones[0:1, 0:1], start=True, stop=True)))(), r=[brk, "ones"], w=["pb1"])
    add("vector", lambda v: v.tensor_copy(bqkT[:, 0:16], pb[1][:, 0:16]), r=["pb1"], w=["bqkT"])
    load_w_quarter(dWk, 0, "Wk")

    def p0_s1(nt):
        par = nt % 2
        sx = stx[0] if par == 0 else _sto
        kxl = ["stx0"] if par == 0 else ["sto0", "sto1"]
        xnb_t = xnb if par == 0 else a_T[:, 0:D]
        kb = "xnb" if par == 0 else "a_T"
        sb_ = 20 * par
        ks = lambda i: f"st{i}:{par}"
        stc = lambda i, n=1: stat[:, sb_ + i:sb_ + i + n]
        add("sync", (lambda nt=nt, sx=sx: (lambda s: s.dma_start(
            out=sx[:, 0:D], in_=dx[128 * nt:128 * (nt + 1), :])))(), w=kxl, dma=True)
        add("vector", (lambda sx=sx, stc=stc: (lambda v: v.bn_stats(
            stc(0, 6), sx[:, 0:512])))(), r=kxl, w=[ks(0)])
        add("vector", (lambda sx=sx, stc=stc: (lambda v: v.bn_stats(
            stc(6, 6), sx[:, 512:1024])))(), r=kxl, w=[ks(106)])
        add("vector", (lambda stc=stc: (lambda v: v.bn_aggr(stc(12, 2), stc(0, 12))))(),
            r=[ks(0), ks(106)], w=[ks(6)])
        add("scalar", (lambda stc=stc: (lambda s: s.activation(stc(14), stc(13), AF.Ln, bias=kons[:, 0:1], scale=1.0)))(),
            r=[ks(6), "kons"], w=[ks(14)])
        add("scalar", (lambda stc=stc: (lambda s: s.activation(stc(14), stc(14), AF.Exp, scale=-0.5)))(),
            r=[ks(14)], w=[ks(14) + "b"])
        add("vector", (lambda stc=stc: (lambda v: v.scalar_tensor_tensor(
            out=stc(15), in0=stc(12), scalar=-1.0, in1=stc(14), op0=ALU.mult, op1=ALU.mult)))(),
            r=[ks(6), ks(14) + "b"], w=[ks(15)])
        add("scalar", (lambda sx=sx, stc=stc, xnb_t=xnb_t, kxl=kxl: (lambda s: s.activation(
            xnb_t[:], sx[:, 0:D], AF.Identity, bias=stc(15), scale=stc(14))))(),
            r=kxl + [ks(15), ks(14) + "b"], w=[kb])

    def p0_s2(nt):
        par = nt % 2
        xnb_t = xnb if par == 0 else a_T[:, 0:D]
        kb = "xnb" if par == 0 else "a_T"
        for td in range(8):
            add("tensor", (lambda td=td, xnb_t=xnb_t, kb=kb: (lambda t: t.transpose(
                pbT[:, 128 * td:128 * (td + 1)], xnb_t[:, 128 * td:128 * (td + 1)], ident[:])))(),
                r=[kb, "ident"], w=["pbT"])
        add("vector", (lambda nt=nt: (lambda v: v.tensor_copy(
            bass.AP(xnT.tensor, 128 * nt, [[xnT.ap[0][0], 128], [T, 8], [1, 128]]),
            pbT.rearrange("p (t c) -> p t c", t=8))))(), r=["pbT"], w=["xnT"])
        if nt % 4 == 3:
            for nt2 in range(nt - 3, nt + 1):
                for td in range(8):
                    for hl, wv, wk_ in ((0, wrdH, "wrdH"), (1, wrdL, "wrdL")):
                        add("tensor", (lambda nt2=nt2, td=td, wv=wv, hl=hl: (lambda t: t.matmul(
                            pb[1][:, nt2:nt2 + 1],
                            xnT[:, T * td + 128 * nt2: T * td + 128 * (nt2 + 1)],
                            wv[:, td:td + 1], start=(td == 0 and hl == 0), stop=(td == 7 and hl == 1))))(),
                            r=["xnT", wk_], w=["pb1"])
            add("vector", (lambda nt=nt: (lambda v: v.tensor_copy(
                dhi[:, nt - 3:nt + 1], pb[1][:, nt - 3:nt + 1])))(), r=["pb1"],
                w=[f"dhi:{i}" for i in range(nt - 3, nt + 1)])
            gen_qkt_nck(QT_OFF, qt_k, 0, nt // 4)

    p0_s1(0)
    for nt in range(1, NT):
        p0_s1(nt)
        p0_s2(nt - 1)
    p0_s2(NT - 1)

    load_w_quarter(dWk, 1, "Wk")
    transpose_w_quarter(0, "WkT")
    load_w_quarter(dWk, 2, "Wk")
    gen_kt_itpair(0)
    transpose_w_quarter(1, "WkT")
    load_w_quarter(dWk, 3, "Wk")
    gen_kt_itpair(2)
    transpose_w_quarter(2, "WkT")
    load_w_quarter(dWv, 0, "Wv")
    gen_kt_itpair(4)
    transpose_w_quarter(3, "WkT")
    load_w_quarter(dWv, 1, "Wv")
    gen_kt_itpair(6)
    add("gpsimd", lambda g: g.dma_start(out=bias_qkv[:], in_=dbv), w=["bias_qkv"], dma=True)
    transpose_w_quarter(0, "WvT")
    load_w_quarter(dWv, 2, "Wv")
    transpose_w_quarter(1, "WvT")
    load_w_quarter(dWv, 3, "Wv")
    for ih in range(2):
        add("tensor", (lambda ih=ih: (lambda t: t.matmul(
            pb[1][:, 0:512], ones[0:1, 0:128],
            bias_qkv[0:1, 512 * ih:512 * (ih + 1)], start=True, stop=True)))(),
            r=["bias_qkv", "ones"], w=["pb1"])
        add("vector", (lambda ih=ih: (lambda v: v.tensor_copy(
            a_nat[:, 512 * ih:512 * (ih + 1)], pb[1][:, 0:512])))(), r=["pb1"], w=["a_nat"])
    for ntl in range(8):
        vgen_one(ntl, 0)
    transpose_w_quarter(2, "WvT")
    load_w_quarter(dWo, 0, "Wo")
    for ntl in range(8, NT):
        vgen_one(ntl, 0)
    transpose_w_quarter(3, "WvT")
    load_w_quarter(dWo, 1, "Wo")
    for ntl in range(8):
        vgen_one(ntl, 1)
    transpose_w_quarter(0, "WoT")
    load_w_quarter(dWo, 2, "Wo")
    for ntl in range(8, NT):
        vgen_one(ntl, 1)
    transpose_w_quarter(1, "WoT")
    load_w_quarter(dWo, 3, "Wo")
    transpose_w_quarter(2, "WoT")
    transpose_w_quarter(3, "WoT")
    add("gpsimd", lambda g: g.dma_start(out=bias_o[:], in_=dbo), w=["bias_o"], dma=True)

    RT = NT
    dhi_keys = [f"dhi:{i}" for i in range(NT)]
    add("vector", lambda v: v.tensor_copy(rtr[:, 0:RT], dhi[:, 0:RT]),
        r=dhi_keys, w=["delta"])
    add("scalar", lambda s: s.activation(rtr[:, 16:16 + RT], rtr[:, 0:RT], AF.Exp, scale=-1.0),
        r=["delta"], w=["re"])
    add("vector", lambda v: v.tensor_scalar_add(rtr[:, 16:16 + RT], rtr[:, 16:16 + RT], 1.0), r=["re"], w=["re1"])
    add("vector", lambda v: v.reciprocal(rtr[:, 32:32 + RT], rtr[:, 16:16 + RT]), r=["re1"], w=["s0"])
    add("vector", lambda v: v.tensor_scalar(out=rtr[:, 48:48 + RT], in0=rtr[:, 0:RT], scalar1=thrc[:, 0:1],
                                            scalar2=None, op0=ALU.is_ge), r=["delta", "thrc"], w=["h0"])
    add("vector", lambda v: v.tensor_add(rtr[:, 64:64 + RT], rtr[:, 48:48 + RT], rtr[:, 32:32 + RT]),
        r=["h0", "s0"], w=["t0"])
    add("vector", lambda v: v.tensor_sub(rtr[:, 64:64 + RT], rtr[:, 64:64 + RT], rtr[:, 32:32 + RT]),
        r=["t0", "s0"], w=["t0b"])
    add("vector", lambda v: v.tensor_mul(w0c[:, 0:RT], rtr[:, 48:48 + RT], rtr[:, 64:64 + RT]),
        r=["h0", "t0b"], w=["w0c"])
    add("vector", lambda v: v.tensor_scalar(out=rtr[:, 48:48 + RT], in0=rtr[:, 48:48 + RT], scalar1=-1.0,
                                            scalar2=1.0, op0=ALU.mult, op1=ALU.add), r=["h0"], w=["h1"])
    add("vector", lambda v: v.tensor_scalar(out=rtr[:, 32:32 + RT], in0=rtr[:, 32:32 + RT], scalar1=-1.0,
                                            scalar2=1.0, op0=ALU.mult, op1=ALU.add), r=["s0"], w=["s1"])
    add("vector", lambda v: v.tensor_add(rtr[:, 64:64 + RT], rtr[:, 48:48 + RT], rtr[:, 32:32 + RT]),
        r=["h1", "s1"], w=["t1"])
    add("vector", lambda v: v.tensor_sub(rtr[:, 64:64 + RT], rtr[:, 64:64 + RT], rtr[:, 32:32 + RT]),
        r=["t1", "s1"], w=["t1b"])
    add("vector", lambda v: v.tensor_mul(w1c[:, 0:RT], rtr[:, 48:48 + RT], rtr[:, 64:64 + RT]),
        r=["h1", "t1b"], w=["w1c"])
    for (wc, wrk, bco, bck) in ((w0c, "w0c", wbc[0], "wbc0"), (w1c, "w1c", wbc[1], "wbc1")):
        add("vector", (lambda wc=wc: (lambda v: v.tensor_copy(wcb[:, 0:NT], wc[:, 0:NT])))(),
            r=[wrk], w=["wcb"])
        for c in range(T // 512):
            for k4 in range(4):
                nt = 4 * c + k4
                add("tensor", (lambda nt=nt, k4=k4: (lambda t: t.transpose(
                    pbT[0:1, 128 * k4:128 * (k4 + 1)], wcb[:, nt:nt + 1], ident[:])))(),
                    r=["wcb", "ident"], w=["pbT"])
            add("vector", lambda v: v.tensor_copy(wrow[0:1, 0:512], pbT[0:1, 0:512]),
                r=["pbT"], w=["wrow"])
            add("tensor", (lambda c=c: (lambda t: t.matmul(
                pb[1][:, 0:512], ones[0:1, 0:128], wrow[0:1, 0:512],
                start=True, stop=True)))(), r=["wrow", "ones"], w=["pb1"])
            add("vector", (lambda bco=bco, c=c, bck=bck: (lambda v: v.tensor_copy(
                bco[:, 512 * c:512 * (c + 1)], pb[1][:, 0:512])))(), r=["pb1"], w=[bck])

    SBANKS = (0, 1, 5)
    def qb_width(qb):
        return 129 if qb == 0 else (257 if qb == 1 else 385)

    def gu_preload(dW_, f_, tag=""):
        sg = stg[f_ % 4]
        gk = f"stg{f_ % 4}"
        add("gpsimd", (lambda dW_=dW_, f_=f_, sg=sg: (lambda g: g.dma_start(
            out=sg.rearrange("p (t f) -> p t f", t=8),
            in_=dW_.rearrange("(t p) f -> p t f", p=128)[:, :, 128 * f_:128 * (f_ + 1)])))(),
            w=[gk], dma=True, tag=tag)

    add("vector", lambda v: v.memset(r0tmp[:], 0.0), w=["r0tmp"])
    for dh in range(2):
        add("tensor", (lambda dh=dh: (lambda t: t.matmul(
            pb[1][:, 0:512], ones[0:1, 0:128],
            bias_o[0:1, 512 * dh:512 * (dh + 1)], start=True, stop=True)))(),
            r=["bias_o", "ones"], w=["pb1"])
        add("vector", (lambda dh=dh: (lambda v: v.tensor_copy(
            xnb[:, 512 * dh:512 * (dh + 1)], pb[1][:, 0:512])))(), r=["pb1"], w=["xnb"])
    qb_order = list(range(1, NT)) + [0]
    pending_tail = None
    for qi, qb in enumerate(qb_order):
        if qb == 0:
            for pq in (1, 2):
                add("sync", (lambda pq=pq: (lambda sy: sy.dma_start(
                    out=stg[pq % 2][:], in_=dascr[128 * pq:128 * (pq + 1), :])))(),
                    r=[f"ascr:{pq}"], w=[f"stg{pq % 2}"], dma=True)
        Wd_ = qb_width(qb)
        msk = mskA if qb == 0 else (mskB if qb == 1 else mskC)
        mkey = "mskA" if qb == 0 else ("mskB" if qb == 1 else "mskC")
        if qb == 0:
            blocks = [(0, 0)]
        elif qb == 1:
            blocks = [(0, 0), (128, 1)]
        else:
            blocks = [(0, 0), (128, qb - 1), (256, qb)]
        cur_kt = qb

        def produce_pair(tpr):
            h0, h1 = 2 * tpr, 2 * tpr + 1
            for bi, (co, kt) in enumerate(blocks):
                for h in (h0, h1):
                    sb = SBANKS[h % 3]
                    pbase = 64 * (h % 2)
                    tih = h // 2
                    add("tensor", (lambda co=co, kt=kt, sb=sb, pbase=pbase, tih=tih, qb=qb: (lambda t: t.matmul(
                        pb[sb][:, co:co + 128],
                        big[pbase:pbase + 64, KT_OFF + T * tih + 128 * kt: KT_OFF + T * tih + 128 * (kt + 1)],
                        big[pbase:pbase + 64, QT_OFF + T * tih + 128 * qb: QT_OFF + T * tih + 128 * (qb + 1)],
                        start=True, stop=True, tile_position=(pbase, 0))))(),
                        r=list(kt_k(tih)) + list(qt_k(tih)), w=[f"pb{sb}"])
            for h in (h0, h1):
                sb = SBANKS[h % 3]
                pbase = 64 * (h % 2)
                tih = h // 2
                add("tensor", (lambda sb=sb, pbase=pbase, tih=tih, cur_kt=cur_kt, Wd_=Wd_: (lambda t: t.matmul(
                    pb[sb][:, Wd_ - 1:Wd_],
                    big[pbase:pbase + 64, KT_OFF + T * tih + 128 * cur_kt: KT_OFF + T * tih + 128 * (cur_kt + 1)],
                    big[pbase:pbase + 64, QT_OFF + T * tih: QT_OFF + T * tih + 1],
                    start=True, stop=True, tile_position=(pbase, 0))))(),
                    r=list(kt_k(tih)) + list(qt_k(tih)), w=[f"pb{sb}"])
            for h in (h0, h1):
                sb = SBANKS[h % 3]
                add("scalar", (lambda sb=sb, h=h, Wd_=Wd_: (lambda s: s.activation(
                    exT[h % 6][:, 0:Wd_], pb[sb][:, 0:Wd_], AF.Exp, scale=0.125)))(),
                    r=[f"pb{sb}"], w=[f"exT{h % 6}"])
                meng = "vector" if h % 2 == 0 else "gpsimd"
                if qb == 1:
                    add(meng, (lambda h=h: (lambda v: v.tensor_mul(
                        exT[h % 6][:, 0:128], exT[h % 6][:, 0:128], mskB[:, 0:128])))(),
                        r=[f"exT{h % 6}", "mskB"], w=[f"exT{h % 6}"])
                    add(meng, (lambda h=h: (lambda v: v.tensor_mul(
                        exT[h % 6][:, 128:257], exT[h % 6][:, 128:257], mskC[:, 256:385])))(),
                        r=[f"exT{h % 6}", "mskC"], w=[f"exT{h % 6}"])
                else:
                    add(meng, (lambda h=h, msk=msk, Wd_=Wd_: (lambda v: v.tensor_mul(
                        exT[h % 6][:, 0:Wd_], exT[h % 6][:, 0:Wd_], msk[:, 0:Wd_])))(),
                        r=[f"exT{h % 6}", mkey], w=[f"exT{h % 6}"])

        def consume_pair(tpr):
            for h in (2 * tpr, 2 * tpr + 1):
                ob = 2 + (h // 7)
                okey = f"pb{ob}"
                ocol = 65 * (h % 7)
                for bi, (co, kt) in enumerate(blocks):
                    add("tensor", (lambda co=co, kt=kt, bi=bi, h=h, ob=ob, ocol=ocol, nb=len(blocks): (lambda t: t.matmul(
                        pb[ob][:, ocol:ocol + 65], exT[h % 6][:, co:co + 128],
                        big[:, V_OFF + 66 * NH * kt + 66 * h: V_OFF + 66 * NH * kt + 66 * h + 65],
                        start=(bi == 0), stop=(bi == nb - 1))))(),
                        r=[f"exT{h % 6}"] + list(v_k(kt)), w=[okey])
                if h < 15:
                    rb_, rc_ = 32 * (h // 5), 130 + 65 * (h % 5)
                else:
                    rb_, rc_ = 0, 130
                add("tensor", (lambda h=h, rb_=rb_, rc_=rc_, cur_kt=cur_kt, Wd_=Wd_: (lambda t: t.matmul(
                    pb[4][rb_:rb_ + 1, rc_:rc_ + 65],
                    exT[h % 6][:, Wd_ - 1:Wd_],
                    big[:, V_OFF + 66 * NH * cur_kt + 66 * h: V_OFF + 66 * NH * cur_kt + 66 * h + 65],
                    start=True, stop=True)))(), r=[f"exT{h % 6}"] + list(v_k(cur_kt)), w=["pb4"])
            if tpr == 0:
                add("vector", lambda v: v.tensor_add(
                    r0tmp[0:1, 0:65], r0tmp[0:1, 0:65], pb[4][0:1, 130:195]),
                    r=["pb4", "r0tmp"], w=["r0tmp"])

        def evac_bank(ob, nh_, rb0):
            add("vector", (lambda ob=ob, nh_=nh_, rb0=rb0: (lambda v: v.reciprocal(
                rsb[:, rb0:rb0 + nh_],
                bass.AP(pb[ob].tensor, pb[ob].offset + 64, [[pb[ob].ap[0][0], 128], [65, nh_]]))))(),
                r=[f"pb{ob}"], w=["rsb"])
            add("vector", (lambda ob=ob, nh_=nh_, rb0=rb0: (lambda v: v.tensor_tensor(
                bass.AP(a_nat.tensor, 64 * rb0, [[a_nat.ap[0][0], 128], [64, nh_], [1, 64]]),
                bass.AP(pb[ob].tensor, pb[ob].offset, [[pb[ob].ap[0][0], 128], [65, nh_], [1, 64]]),
                bass.AP(rsb.tensor, rb0, [[rsb.ap[0][0], 128], [1, nh_], [0, 64]]),
                ALU.mult)))(), r=[f"pb{ob}", "rsb"], w=["a_nat"])

        produce_pair(0)
        produce_pair(1)
        produce_pair(2)
        if pending_tail is not None:
            pending_tail()
            pending_tail = None
        for tpr in range(3, 8):
            consume_pair(tpr - 3)
            if tpr == 7:
                evac_bank(2, 7, 0)
            produce_pair(tpr)
        consume_pair(5)
        consume_pair(6)
        evac_bank(3, 7, 7)
        consume_pair(7)
        evac_bank(4, 2, 14)
        add("vector", lambda v: v.tensor_add(
            r0tmp[0:1, 65:325], r0tmp[0:1, 65:325], pb[4][0:1, 195:455]),
            r=["pb4", "r0tmp"], w=["r0tmp"])
        add("vector", lambda v: v.tensor_add(
            r0tmp[0:1, 325:390], r0tmp[0:1, 325:390], pb[4][0:1, 130:195]),
            r=["pb4", "r0tmp"], w=["r0tmp"])
        for rb_ in (32, 64):
            add("vector", (lambda rb_=rb_: (lambda v: v.tensor_add(
                r0tmp[rb_:rb_ + 1, 0:325], r0tmp[rb_:rb_ + 1, 0:325],
                pb[4][rb_:rb_ + 1, 130:455])))(), r=["pb4", "r0tmp"], w=["r0tmp"])
        if qb == 0:
            for rb_, ncnt in ((0, 6), (32, 5), (64, 5)):
                add("vector", (lambda rb_=rb_, ncnt=ncnt: (lambda v: v.reciprocal(
                    bass.AP(r0tmp.tensor, 468 * rb_ + 420, [[468, 1], [1, ncnt]]),
                    bass.AP(r0tmp.tensor, 468 * rb_ + 64, [[468, 1], [65, ncnt]]))))(),
                    r=["r0tmp"], w=["r0tmp"])
                add("vector", (lambda rb_=rb_, ncnt=ncnt: (lambda v: v.tensor_tensor(
                    bass.AP(r0bf.tensor, r0bf.ap[0][0] * rb_, [[r0bf.ap[0][0], 1], [64, ncnt], [1, 64]]),
                    bass.AP(r0tmp.tensor, 468 * rb_, [[468, 1], [65, ncnt], [1, 64]]),
                    bass.AP(r0tmp.tensor, 468 * rb_ + 420, [[468, 1], [1, ncnt], [0, 64]]), ALU.mult)))(),
                    r=["r0tmp"], w=["r0bf"])
            add("sync", lambda sy: sy.dma_start(
                out=a_nat[0:1, 0:320], in_=r0bf[0:1, 0:320]),
                r=["r0bf"], w=["a_nat"], dma=True)
            add("sync", lambda sy: sy.dma_start(
                out=a_nat[0:1, 960:1024], in_=r0bf[0:1, 320:384]),
                r=["r0bf"], w=["a_nat"], dma=True)
            add("sync", lambda sy: sy.dma_start(
                out=a_nat[0:1, 320:640], in_=r0bf[32:33, 0:320]),
                r=["r0bf"], w=["a_nat"], dma=True)
            add("sync", lambda sy: sy.dma_start(
                out=a_nat[0:1, 640:960], in_=r0bf[64:65, 0:320]),
                r=["r0bf"], w=["a_nat"], dma=True)
        def make_tail(qb=qb):
            def tail():
                for g4 in range(2):
                    dst = pbT if g4 == 0 else pbU
                    dkey = "pbT" if g4 == 0 else "pbU"
                    for t4 in range(4):
                        ti = 4 * g4 + t4
                        add("tensor", (lambda ti=ti, t4=t4, dst=dst, dkey=dkey: (lambda t: t.transpose(
                            dst[:, 128 * t4:128 * (t4 + 1)],
                            a_nat[:, 128 * ti:128 * (ti + 1)], ident[:])))(),
                            r=["a_nat", "ident"], w=[dkey])
                    if g4 == 0:
                        add("scalar", (lambda g4=g4: (lambda s: s.copy(
                            bass.AP(a_T.tensor, 512 * g4, [[a_T.ap[0][0], 128], [128, 4], [1, 128]]),
                            pbT[:, 0:512].rearrange("p (a b) -> p a b", a=4))))(),
                            r=["pbT"], w=["a_T"])
                    else:
                        add("vector", (lambda g4=g4: (lambda v: v.tensor_copy(
                            bass.AP(a_T.tensor, 512 * g4, [[a_T.ap[0][0], 128], [128, 4], [1, 128]]),
                            pbU[:, 0:512].rearrange("p (a b) -> p a b", a=4))))(),
                            r=["pbU"], w=["a_T"])
                if qb != 0:
                    add("sync", (lambda qb=qb: (lambda s: s.dma_start(
                        out=dascr[128 * qb:128 * (qb + 1), :], in_=a_T[:])))(),
                        r=["a_T"], w=[f"ascr:{qb}"], dma=True)
            return tail
        pending_tail = make_tail()

    gu_preload(dWg[0], 2, "g0pre")
    gu_preload(dWg[0], 3, "g0pre")
    for qb in list(range(1, NT)):
        sg = stg[qb % 2]
        gk = f"stg{qb % 2}"
        if qb > 2:
            add("sync", (lambda qb=qb, sg=sg: (lambda s: s.dma_start(
                out=sg[:], in_=dascr[128 * qb:128 * (qb + 1), :])))(),
                r=[f"ascr:{qb}"], w=[gk], dma=True)
        for dh in range(2):
            bank = [5, 4, 3, 2][(2 * (qb - 1) + dh) % 4]
            pkey = f"pb{bank}"
            for ti in range(8):
                add("tensor", (lambda ti=ti, dh=dh, bank=bank, sg=sg: (lambda t: t.matmul(
                    pb[bank][:, 0:512], sg[:, 128 * ti:128 * (ti + 1)],
                    ovl[:, WT_OFF + 1024 * ti + 512 * dh:WT_OFF + 1024 * ti + 512 * (dh + 1)],
                    start=(ti == 0), stop=(ti == 7))))(),
                    r=[gk] + [wtI(4 * dh + j) for j in range(4)], w=[pkey])
            add("vector", (lambda dh=dh, bank=bank: (lambda v: v.tensor_tensor(
                sto[dh][:], pb[bank][:, 0:512],
                xnb[:, 512 * dh:512 * (dh + 1)], ALU.add)))(),
                r=[pkey, "xnb"], w=[f"sto{dh}"])
            add("gpsimd", (lambda qb=qb, dh=dh: (lambda g: g.dma_start(
                out=dpscr[128 * qb:128 * (qb + 1), 512 * dh:512 * (dh + 1)],
                in_=sto[dh][:])))(), r=[f"sto{dh}"], w=[f"pscr:{qb}"], dma=True)
        if qb == 14:
            gu_preload(dWg[0], 0, "g0pre")
        if qb == 15:
            gu_preload(dWg[0], 1, "g0pre")

    pending_tail()
    pending_tail = None
    for dh in range(2):
        bank = [5, 4, 3, 2][(30 + dh) % 4]
        pkey = f"pb{bank}"
        for ti in range(8):
            add("tensor", (lambda ti=ti, dh=dh, bank=bank: (lambda t: t.matmul(
                pb[bank][:, 0:512], a_T[:, 128 * ti:128 * (ti + 1)],
                ovl[:, WT_OFF + 1024 * ti + 512 * dh:WT_OFF + 1024 * ti + 512 * (dh + 1)],
                start=(ti == 0), stop=(ti == 7))))(),
                r=["a_T"] + [wtI(4 * dh + j) for j in range(4)], w=[pkey])
        add("vector", (lambda dh=dh, bank=bank: (lambda v: v.tensor_tensor(
            sto[dh][:], pb[bank][:, 0:512],
            xnb[:, 512 * dh:512 * (dh + 1)], ALU.add)))(),
            r=[pkey, "xnb"], w=[f"sto{dh}"])
        add("gpsimd", (lambda dh=dh: (lambda g: g.dma_start(
            out=dpscr[0:128, 512 * dh:512 * (dh + 1)],
            in_=sto[dh][:])))(), r=[f"sto{dh}"], w=["pscr:0"], dma=True)


    def gu_G(dWg_, tag, skip_loads=0, inject=None, forder=None):
        for fi, f_ in enumerate(forder if forder is not None else range(16)):
            if inject and fi in inject:
                inject[fi]()
            sg = stg[f_ % 4]
            gk = f"stg{f_ % 4}"
            if fi >= skip_loads:
                gu_preload(dWg_, f_, tag)
            for ncs in range(NC512):
                bank = ncs % 2
                pkey = f"pb{bank}"
                for td in range(8):
                    add("tensor", (lambda f_=f_, ncs=ncs, td=td, bank=bank, sg=sg: (lambda t: t.matmul(
                        pb[bank][:, 0:512], sg[:, 128 * td:128 * (td + 1)],
                        xnT[:, T * td + 512 * ncs:T * td + 512 * (ncs + 1)],
                        start=(td == 0), stop=(td == 7))))(), r=[gk, "xnT"], w=[pkey])
                add("scalar", (lambda f_=f_, ncs=ncs, bank=bank: (lambda s: s.activation(
                    big[:, H_OFF + T * f_ + 512 * ncs:H_OFF + T * f_ + 512 * (ncs + 1)],
                    pb[bank][:, 0:512], AF.Silu)))(), r=[pkey], w=list(h_k(f_)))
    def gu_U(dWu_, e_idx, tag, inject=None):
        for f_ in range(16):
            if inject and f_ in inject:
                inject[f_]()
            sg = stg[f_ % 4]
            gk = f"stg{f_ % 4}"
            gu_preload(dWu_, f_, tag)
            for ncs in range(NC512):
                bank = 2 + ncs % 2
                pkey = f"pb{bank}"
                for td in range(8):
                    add("tensor", (lambda f_=f_, ncs=ncs, td=td, bank=bank, sg=sg: (lambda t: t.matmul(
                        pb[bank][:, 0:512], sg[:, 128 * td:128 * (td + 1)],
                        xnT[:, T * td + 512 * ncs:T * td + 512 * (ncs + 1)],
                        start=(td == 0), stop=(td == 7))))(), r=[gk, "xnT"], w=[pkey])
                add("vector", (lambda f_=f_, ncs=ncs, bank=bank: (lambda v: v.tensor_mul(
                    big[:, H_OFF + T * f_ + 512 * ncs:H_OFF + T * f_ + 512 * (ncs + 1)],
                    big[:, H_OFF + T * f_ + 512 * ncs:H_OFF + T * f_ + 512 * (ncs + 1)],
                    pb[bank][:, 0:512])))(), r=[pkey] + list(h_k(f_)), w=list(h_k(f_)))
                if e_idx is not None:
                    add("vector", (lambda f_=f_, ncs=ncs, e_idx=e_idx: (lambda v: v.tensor_mul(
                        big[:, H_OFF + T * f_ + 512 * ncs:H_OFF + T * f_ + 512 * (ncs + 1)],
                        big[:, H_OFF + T * f_ + 512 * ncs:H_OFF + T * f_ + 512 * (ncs + 1)],
                        wbc[e_idx][:, 512 * ncs:512 * (ncs + 1)])))(),
                        r=[f"wbc{e_idx}"] + list(h_k(f_)), w=list(h_k(f_)))

    def wdc_load(dWd_, dh, buf, c, tag):
        off = (WDC_OFF if buf == 0 else WDC2_OFF) + 2048 * c
        key = f"big:{off // 2048}"
        add("gpsimd", (lambda dWd_=dWd_, dh=dh, off=off, c=c: (lambda g: g.dma_start(
            out=bass.AP(big.tensor, off, [[PSTEP_B, 128], [512, 4], [1, 512]]),
            in_=dWd_.rearrange("(t p) d -> p t d", p=128)[:, 4 * c:4 * (c + 1), 512 * dh:512 * (dh + 1)])))(),
            w=[key], dma=True, tag=tag)

    def d2_loads(ntile, dh):
        sl = ntile % 4
        sxl = ntile % 2
        sx = stxr[:, 512 * sxl:512 * (sxl + 1)]
        add("scalar", (lambda ntile=ntile, dh=dh, sx=sx: (lambda s: s.dma_start(
            out=sx,
            in_=dx[128 * ntile:128 * (ntile + 1), 512 * dh:512 * (dh + 1)])))(),
            w=[f"sx{sxl}"], dma=True)
        pp = stg[sl][:, 0:512]
        add("scalar", (lambda ntile=ntile, dh=dh, pp=pp: (lambda s: s.dma_start(
            out=pp,
            in_=dpscr[128 * ntile:128 * (ntile + 1), 512 * dh:512 * (dh + 1)])))(),
            r=[f"pscr:{ntile}"], w=[f"stg{sl}"], dma=True)

    def d_stage_half(mode, dh, buf, tag):
        woff = WDC_OFF if buf == 0 else WDC2_OFF
        if mode == 2:
            for nt_ in range(2):
                d2_loads(nt_, dh)
        for g8 in range(T // 256):
            grp = g8 % 3
            banks = (2 * grp, 2 * grp + 1)
            for f_ in range(16):
                for sub in range(2):
                    ntile = 2 * g8 + sub
                    bkv = banks[sub]
                    add("tensor", (lambda f_=f_, ntile=ntile, bk=bkv, woff=woff: (lambda t: t.matmul(
                        pb[bk][:, 0:512],
                        big[:, H_OFF + T * f_ + 128 * ntile:H_OFF + T * f_ + 128 * (ntile + 1)],
                        big[:, woff + 512 * f_:woff + 512 * (f_ + 1)],
                        start=(f_ == 0), stop=(f_ == 15))))(),
                        r=list(h_k(f_)) + [f"big:{(woff + 512 * f_) // 2048}"], w=[f"pb{bkv}"])
            for sub in range(2):
                ntile = 2 * g8 + sub
                so = sto[ntile % 2]
                sok = f"sto{ntile % 2}"
                pkey = f"pb{banks[sub]}"
                if mode == 2:
                    sl = ntile % 4
                    sxl = ntile % 2
                    sx = stxr[:, 512 * sxl:512 * (sxl + 1)]
                    pp = stg[sl][:, 0:512]
                    add("vector", (lambda bk=banks[sub], so=so, sx=sx: (lambda v: v.tensor_add(
                        so[:], pb[bk][:, 0:512], sx)))(), r=[pkey, f"sx{sxl}"], w=[sok])
                    add("vector", (lambda so=so, pp=pp: (lambda v: v.tensor_add(
                        so[:], so[:], pp)))(), r=[sok, f"stg{sl}"], w=[sok])
                else:
                    add("vector", (lambda bk=banks[sub], so=so: (lambda v: v.tensor_copy(
                        so[:], pb[bk][:, 0:512])))(), r=[pkey], w=[sok])
                okey = f"out:{ntile}:{dh}"
                if mode == 0:
                    add("gpsimd", (lambda so=so, ntile=ntile, dh=dh: (lambda g: g.dma_start(
                        out=dout[128 * ntile:128 * (ntile + 1), 512 * dh:512 * (dh + 1)],
                        in_=so[:])))(), r=[sok, okey], w=[okey], dma=True, tag=tag)
                else:
                    add("gpsimd", (lambda so=so, ntile=ntile, dh=dh: (lambda g: g.dma_start(
                        out=dout[128 * ntile:128 * (ntile + 1), 512 * dh:512 * (dh + 1)],
                        in_=so[:], accum_op=ALU.add)))(), r=[sok, okey], w=[okey], dma=True, tag=tag)
            if mode == 2:
                for nt_ in (2 * g8 + 2, 2 * g8 + 3):
                    if nt_ < 16:
                        d2_loads(nt_, dh)

    def wdc_injects(dWd_, tag):
        inj = {}
        for c in range(4):
            inj[1 + 2 * c] = (lambda dWd_=dWd_, c=c: wdc_load(dWd_, 0, 0, c, tag + "a"))
            inj[8 + 2 * c] = (lambda dWd_=dWd_, c=c: wdc_load(dWd_, 1, 1, c, tag + "b"))
        return inj

    gu_G(dWg[0], "g0", skip_loads=4, forder=[2, 3, 0, 1] + list(range(4, 16)))
    gu_U(dWu[0], 0, "u0", inject=wdc_injects(dWd[0], "wd0"))
    for _pf in range(4):
        gu_preload(dWg[1], _pf, "pre1")
    d_stage_half(0, 0, 0, "d0h0")
    d_stage_half(0, 1, 1, "d0h1")
    gu_G(dWg[1], "g1", skip_loads=4)
    gu_U(dWu[1], 1, "u1", inject=wdc_injects(dWd[1], "wd1"))
    for _pf in range(4):
        gu_preload(dSg, _pf, "pres")
    d_stage_half(1, 0, 0, "d1h0")
    d_stage_half(1, 1, 1, "d1h1")
    gu_G(dSg, "gs", skip_loads=4)
    gu_U(dSu, None, "us", inject=wdc_injects(dSd, "sd"))
    d_stage_half(2, 0, 0, "dsh0")
    d_stage_half(2, 1, 1, "dsh1")

    B.finalize(nc)
    fc = B.final_counts
    B.emit(nc)
    with nc.Block() as blk:
        @blk.gpsimd
        def _(g):
            for e in ("gpsimd", "sync", "scalar"):
                n = B.qcnt[e]
                for sl in range(B.R):
                    full = 16 * ((n - sl + B.R - 1) // B.R) if n > 0 else 0
                    if full > 0:
                        g.wait_ge(B.sems[("q", e, sl)], full)
    return nc


def kernel(L0_attn_x, L0_attn_Wq, L0_attn_bq, L0_attn_Wk, L0_attn_bk, L0_attn_Wv,
           L0_attn_bv, L0_attn_Wo, L0_attn_bo, L0_ffn_W_router, L0_ffn_router_bias,
           L0_ffn_Wg, L0_ffn_Wu, L0_ffn_Wd, L0_ffn_Sg, L0_ffn_Su, L0_ffn_Sd):
    T = 2048
    x = np.ascontiguousarray(np.asarray(L0_attn_x, dtype=np.float32))
    Bsz = x.shape[0]
    mA, mB, mC = _masks()
    com = {
        "Wq": np.ascontiguousarray(L0_attn_Wq, dtype=np.float32),
        "bq": np.ascontiguousarray(np.asarray(L0_attn_bq, dtype=np.float32).reshape(1, D)),
        "Wk": np.ascontiguousarray(L0_attn_Wk, dtype=np.float32),
        "bk": np.ascontiguousarray(np.asarray(L0_attn_bk, dtype=np.float32).reshape(1, D)),
        "Wv": np.ascontiguousarray(L0_attn_Wv, dtype=np.float32),
        "bv": np.ascontiguousarray(np.asarray(L0_attn_bv, dtype=np.float32).reshape(1, D)),
        "Wo": np.ascontiguousarray(L0_attn_Wo, dtype=np.float32),
        "bo": np.ascontiguousarray(np.asarray(L0_attn_bo, dtype=np.float32).reshape(1, D)),
        "Wr": np.ascontiguousarray(L0_ffn_W_router, dtype=np.float32),
        "rb": np.ascontiguousarray(np.asarray(L0_ffn_router_bias, dtype=np.float32).reshape(1, NE)),
        "Wg0": np.ascontiguousarray(np.asarray(L0_ffn_Wg, dtype=np.float32)[0]),
        "Wg1": np.ascontiguousarray(np.asarray(L0_ffn_Wg, dtype=np.float32)[1]),
        "Wu0": np.ascontiguousarray(np.asarray(L0_ffn_Wu, dtype=np.float32)[0]),
        "Wu1": np.ascontiguousarray(np.asarray(L0_ffn_Wu, dtype=np.float32)[1]),
        "Wd0": np.ascontiguousarray(np.asarray(L0_ffn_Wd, dtype=np.float32)[0]),
        "Wd1": np.ascontiguousarray(np.asarray(L0_ffn_Wd, dtype=np.float32)[1]),
        "Sg": np.ascontiguousarray(L0_ffn_Sg, dtype=np.float32),
        "Su": np.ascontiguousarray(L0_ffn_Su, dtype=np.float32),
        "Sd": np.ascontiguousarray(L0_ffn_Sd, dtype=np.float32),
        "mA": mA, "mB": mB, "mC": mC,
        "ident": np.eye(128, dtype=np.float32),
    }
    nc = build_program(T)
    in_maps = []
    for b in range(Bsz):
        m = dict(com)
        m["x"] = np.ascontiguousarray(x[b])
        in_maps.append(m)
    res = run_bass_kernel_spmd(nc, in_maps, core_ids=list(range(Bsz)))
    out = np.stack([np.asarray(r["out"], dtype=np.float32) for r in res.results], axis=0)
    return out


# version 34
# speedup vs baseline: 1.0225x; 1.0161x over previous
import numpy as np
import concourse.bass as bass
import concourse.mybir as mybir
from concourse.bass_utils import run_bass_kernel_spmd

F32 = mybir.dt.float32
BF16 = mybir.dt.bfloat16
AF = mybir.ActivationFunctionType
ALU = mybir.AluOpType
AX = mybir.AxisListType

D = 1024
NH = 16
HD = 64
F = 2048
NE = 2
EPS = 1e-5

ENGINES = ("tensor", "vector", "scalar", "gpsimd", "sync")


class _Op:
    __slots__ = ("eng", "fn", "reads", "writes", "is_dma", "inc", "sem_val",
                 "deps", "waits", "snap", "tag", "slot", "gen")

    def __init__(self, eng, fn, reads, writes, is_dma, tag):
        self.eng = eng
        self.fn = fn
        self.reads = tuple(reads)
        self.writes = tuple(writes)
        self.is_dma = is_dma
        self.inc = is_dma
        self.sem_val = None
        self.deps = set()
        self.waits = []
        self.snap = None
        self.tag = tag


class Builder:
    def __init__(self):
        self.ops = {e: [] for e in ENGINES}
        self.glob = []
        self.state = {}

    def add(self, eng, fn, r=(), w=(), dma=False, tag=""):
        rpb = tuple(k for k in r if k.startswith("pb"))
        if rpb:
            r = tuple(k for k in r if not k.startswith("pb"))
            w = tuple(w) + rpb
        op = _Op(eng, fn, r, w, dma, tag)
        idx = len(self.ops[eng])
        deps = set()
        for k in op.reads:
            st = self.state.get(k)
            if st and st["w"] is not None:
                deps.add(st["w"])
        for k in op.writes:
            st = self.state.get(k)
            if st:
                if st["w"] is not None:
                    deps.add(st["w"])
                deps.update(st["r"])
        op.deps = {d for d in deps if not (d[0] == eng == "tensor" and not dma
                                           and not self.ops[eng][d[1]].is_dma)}
        self.ops[eng].append(op)
        self.glob.append(op)
        ref = (eng, idx)
        for k in op.reads:
            self.state.setdefault(k, {"w": None, "r": []})["r"].append(ref)
        for k in op.writes:
            self.state[k] = {"w": ref, "r": []}
        return op

    def finalize(self, nc):
        R = 8
        self.R = R
        for op in self.glob:
            for (e, j) in op.deps:
                self.ops[e][j].inc = True
        sems = {}
        for e in ENGINES:
            sems[("c", e)] = nc.semaphore(f"sc_{e}").__enter__()
        for e in ("gpsimd", "sync", "scalar"):
            for sl in range(R):
                sems[("q", e, sl)] = nc.semaphore(f"q_{e}_{sl}").__enter__()
        cum = {e: 0 for e in ENGINES}
        qcnt = {e: 0 for e in ENGINES}
        for e in ENGINES:
            for op in self.ops[e]:
                if op.is_dma:
                    i = qcnt[e]
                    qcnt[e] = i + 1
                    op.slot = (i % R)
                    op.gen = (i // R)
                    op.sem_val = 16 * (op.gen + 1)
                elif op.inc:
                    cum[e] += 1
                    op.sem_val = cum[e]
        self.final_counts = {("c", e): cum[e] for e in ENGINES}
        self.qcnt = qcnt
        wm = {e: {} for e in ENGINES}

        def semkey(op):
            if op.is_dma:
                return ("q", op.eng, op.slot)
            return ("c", op.eng)

        for op in self.glob:
            e = op.eng
            needed = []
            for (e2, j2) in op.deps:
                p = self.ops[e2][j2]
                k = semkey(p)
                if wm[e].get(k, 0) >= p.sem_val:
                    continue
                needed.append((k, p.sem_val, p))
            if op.is_dma and op.gen > 0:
                k = ("q", e, op.slot)
                if wm[e].get(k, 0) < 16 * op.gen:
                    needed.append((k, 16 * op.gen, None))
            for (k, v, p) in sorted(needed, key=lambda t: (t[0], t[1])):
                if wm[e].get(k, 0) >= v:
                    continue
                op.waits.append((k, v))
                wm[e][k] = v
                if p is not None and p.snap:
                    for k2, v2 in p.snap.items():
                        if wm[e].get(k2, 0) < v2:
                            wm[e][k2] = v2
            if op.inc:
                snap = dict(wm[e])
                k = semkey(op)
                snap[k] = max(snap.get(k, 0), op.sem_val)
                op.snap = snap
        self.sems = sems

    def emit(self, nc):
        sems = self.sems
        with nc.Block() as block:
            for e in ENGINES:
                ops = self.ops[e]

                def make(eng_ops, ename):
                    def run(eng):
                        for op in eng_ops:
                            for (k, v) in op.waits:
                                eng.wait_ge(sems[k], v)
                            inst = op.fn(eng)
                            if op.is_dma:
                                inst.then_inc(sems[("q", ename, op.slot)], 16)
                            elif op.inc:
                                inst.then_inc(sems[("c", ename)], 1)
                    return run

                getattr(block, e)(make(ops, e))


def _masks():
    p = np.arange(128)[:, None]
    j = np.arange(128)[None, :]
    low = (p <= j).astype(np.float32)
    up = (p >= j).astype(np.float32)
    col0 = np.broadcast_to((p == 0), (128, 128)).astype(np.float32)
    one = np.ones((128, 1), np.float32)
    mA = np.ascontiguousarray(np.concatenate([np.maximum(low, col0), one], axis=1))
    mB = np.ascontiguousarray(np.maximum(up, col0))
    mC = np.ascontiguousarray(np.concatenate([col0, up, low, one], axis=1))
    return mA, mB, mC


def build_program(T=2048):
    NT = T // 128
    NC512 = T // 512
    nc = bass.Bass("TRN2", target_bir_lowering=False)
    B = Builder()
    add = B.add

    dx = nc.dram_tensor("x", [T, D], F32, kind="ExternalInput").ap()
    dWq = nc.dram_tensor("Wq", [D, D], F32, kind="ExternalInput").ap()
    dbq = nc.dram_tensor("bq", [1, D], F32, kind="ExternalInput").ap()
    dWk = nc.dram_tensor("Wk", [D, D], F32, kind="ExternalInput").ap()
    dbk = nc.dram_tensor("bk", [1, D], F32, kind="ExternalInput").ap()
    dWv = nc.dram_tensor("Wv", [D, D], F32, kind="ExternalInput").ap()
    dbv = nc.dram_tensor("bv", [1, D], F32, kind="ExternalInput").ap()
    dWo = nc.dram_tensor("Wo", [D, D], F32, kind="ExternalInput").ap()
    dbo = nc.dram_tensor("bo", [1, D], F32, kind="ExternalInput").ap()
    dWr = nc.dram_tensor("Wr", [D, NE], F32, kind="ExternalInput").ap()
    drb = nc.dram_tensor("rb", [1, NE], F32, kind="ExternalInput").ap()
    dWg = [nc.dram_tensor(f"Wg{e}", [D, F], F32, kind="ExternalInput").ap() for e in range(NE)]
    dWu = [nc.dram_tensor(f"Wu{e}", [D, F], F32, kind="ExternalInput").ap() for e in range(NE)]
    dWd = [nc.dram_tensor(f"Wd{e}", [F, D], F32, kind="ExternalInput").ap() for e in range(NE)]
    dSg = nc.dram_tensor("Sg", [D, F], F32, kind="ExternalInput").ap()
    dSu = nc.dram_tensor("Su", [D, F], F32, kind="ExternalInput").ap()
    dSd = nc.dram_tensor("Sd", [F, D], F32, kind="ExternalInput").ap()
    dmA = nc.dram_tensor("mA", [128, 129], F32, kind="ExternalInput").ap()
    dmB = nc.dram_tensor("mB", [128, 128], F32, kind="ExternalInput").ap()
    dmC = nc.dram_tensor("mC", [128, 385], F32, kind="ExternalInput").ap()
    dident = nc.dram_tensor("ident", [128, 128], F32, kind="ExternalInput").ap()
    dout = nc.dram_tensor("out", [T, D], F32, kind="ExternalOutput").ap()
    dpscr = nc.dram_tensor("pscr", [T, D], BF16, kind="Internal").ap()
    dascr = nc.dram_tensor("ascr", [T, D], BF16, kind="Internal").ap()

    S = lambda n, sh, dt: nc.alloc_sbuf_tensor(n, sh, dt).ap()
    xnT = S("xnT", [128, 8 * T], BF16)
    BIGC = max(16 * T + NT * 66 * NH, 16 * F + 2 * 16 * 512)
    big = S("big", [128, BIGC], BF16)
    QT_OFF, KT_OFF, V_OFF = 0, 8 * T, 16 * T
    H_OFF, WDC_OFF, WDC2_OFF = 0, 16 * F, 16 * F + 8192
    ovl = S("ovl", [128, 12288], BF16)
    WN_OFF, WT_OFF = 0, 4096
    stx = [S("stx0", [128, 1024], F32)]
    stg = [S(f"stg{i}", [128, 1024], BF16) for i in range(4)]
    _sto = S("sto", [128, 1024], F32)
    sto = [_sto[:, 0:512], _sto[:, 512:1024]]
    stxr = S("stxr", [128, 1024], F32)
    wbc = [S(f"wbc{e}", [128, T], BF16) for e in range(NE)]
    exT = [S(f"exT{i}", [128, 385], BF16) for i in range(6)]
    a_nat = S("a_nat", [128, 1024], BF16)
    a_T = S("a_T", [128, 1024], BF16)
    mskA = S("mskA", [128, 129], BF16)
    mskB = S("mskB", [128, 128], BF16)
    mskC = S("mskC", [128, 385], BF16)
    ident = S("ident_s", [128, 128], BF16)
    bias_qkv = S("bias_qkv", [1, D], BF16)
    bias_qkv2 = S("bias_qkv2", [1, D], BF16)
    bias_o = S("bias_o", [1, D], BF16)
    bqkT = S("bqkT", [128, 16], F32)
    r0bf = S("r0bf", [128, 448], BF16)
    ones = S("ones_s", [1, 512], BF16)
    stat = S("stat", [128, 80], F32)
    kons = S("kons", [128, 1], F32)
    xnb = S("xnb", [128, D], BF16)
    wrc = S("wrc", [128, 16], F32)
    wrdf = S("wrdf", [128, 8], F32)
    wrdH = S("wrdH", [128, 8], BF16)
    wrdL = S("wrdL", [128, 8], BF16)
    wrow = S("wrow", [1, 1024], BF16)
    dhi = S("dhi", [128, 16], F32)
    thrc = S("thrc", [128, 1], F32)
    rbs = S("rbs", [1, 2], F32)
    rbt = S("rbt", [1, 2], F32)
    rbtb = S("rbtb", [1, 2], BF16)
    rtr = S("rtr", [128, 16 * 5], F32)
    w0c = S("w0c", [128, 16], F32)
    w1c = S("w1c", [128, 16], F32)
    wcb = S("wcb", [128, 16], BF16)
    rsb = S("rsb", [128, 16], F32)
    r0tmp = S("r0tmp", [128, 468], F32)
    wrdbc = wbc[0][:, 0:1024] if T >= 1024 else S("wrdbc", [128, 1024], BF16)[:]
    stp = stg[0][:, 0:512]

    P = lambda n, sh, dt: nc.alloc_psum_tensor(n, sh, dt).ap()
    pb = [P(f"pb{i}", [128, 512], F32) for i in range(6)]
    pbU = P("pbU", [128, 1024], BF16)
    pbT = P("pbT", [128, 1024], BF16)
    PSTEP_B = big.ap[0][0]
    PSTEP_T = pbT.ap[0][0]

    def big_keys(lo_col, hi_col):
        return tuple(f"big:{c}" for c in range(lo_col // 2048, (hi_col + 2047) // 2048))

    qt_k = lambda it: big_keys(QT_OFF + T * it, QT_OFF + T * (it + 1))
    kt_k = lambda it: big_keys(KT_OFF + T * it, KT_OFF + T * (it + 1))
    v_k = lambda ntl: big_keys(V_OFF + 66 * NH * ntl, V_OFF + 66 * NH * (ntl + 1))
    h_k = lambda f: big_keys(H_OFF + T * f, H_OFF + T * (f + 1))

    wn_k = ("ovl:wn",)
    wtI = lambda ti: f"wtI{ti}"

    add("gpsimd", lambda g: g.dma_start(out=ident[:], in_=dident), w=["ident"], dma=True)
    add("gpsimd", lambda g: g.dma_start(out=mskA[:], in_=dmA), w=["mskA"], dma=True)
    add("gpsimd", lambda g: g.dma_start(out=mskB[:], in_=dmB), w=["mskB"], dma=True)
    add("gpsimd", lambda g: g.dma_start(out=mskC[:], in_=dmC), w=["mskC"], dma=True)
    add("vector", lambda v: v.memset(ones[:], 1.0), w=["ones"])
    add("vector", lambda v: v.memset(kons[:, 0:1], EPS), w=["kons"])
    vones = bass.AP(big.tensor, V_OFF + 64, [[PSTEP_B, 128], [66, NH * NT], [1, 1]])

    add("sync", lambda s: s.dma_start(out=rbs[:], in_=drb), w=["rbs"], dma=True)
    add("vector", lambda v: v.tensor_sub(rbt[0:1, 0:1], rbs[0:1, 1:2], rbs[0:1, 0:1]), r=["rbs"], w=["rbt0"])
    add("vector", lambda v: v.tensor_copy(rbtb[0:1, 0:1], rbt[0:1, 0:1]), r=["rbt0"], w=["rbtb0"])
    add("vector", lambda v: v.tensor_sub(rbt[0:1, 1:2], rbt[0:1, 0:1], rbtb[0:1, 0:1]), r=["rbt0", "rbtb0"], w=["rbt1"])
    add("vector", lambda v: v.tensor_copy(rbtb[0:1, 1:2], rbt[0:1, 1:2]), r=["rbt1"], w=["rbtb1"])
    add("sync", lambda s: s.dma_start(out=wrc.rearrange("p (t e) -> p t e", t=8),
                                      in_=dWr.rearrange("(t p) e -> p t e", p=128)), w=["wrc"], dma=True)
    add("vector", lambda v: v.tensor_sub(wrdf[:], wrc[:, 0:16:2], wrc[:, 1:16:2]), r=["wrc"], w=["wrdf"])
    add("vector", lambda v: v.tensor_copy(wrdH[:], wrdf[:]), r=["wrdf"], w=["wrdH"])
    add("vector", lambda v: v.tensor_sub(wrdL[:], wrdf[:], wrdH[:]), r=["wrdf", "wrdH"], w=["wrdL"])
    add("tensor", lambda t: t.matmul(pb[1][:, 32:33], ones[0:1, 0:128], rbtb[0:1, 0:1], start=True, stop=False),
        r=["rbtb0", "ones"], w=["pb1"])
    add("tensor", lambda t: t.matmul(pb[1][:, 32:33], ones[0:1, 0:128], rbtb[0:1, 1:2], start=False, stop=True),
        r=["rbtb1", "ones"], w=["pb1"])
    add("vector", lambda v: v.tensor_copy(thrc[:], pb[1][:, 32:33]), r=["pb1"], w=["thrc"])
    for t_ in range(8):
        add("tensor", (lambda t_=t_: (lambda tt: tt.transpose(
            pbT[0:1, 0:128], wrdH[:, t_:t_ + 1], ident[:])))(), r=["wrdH", "ident"], w=["pbT"])
        add("vector", (lambda t_=t_: (lambda v: v.tensor_copy(
            wrow[0:1, 128 * t_:128 * (t_ + 1)], pbT[0:1, 0:128])))(), r=["pbT"], w=["wrow"])
    for c in range(2):
        add("tensor", (lambda c=c: (lambda t: t.matmul(
            pb[1][:, 0:512], ones[0:1, 0:128], wrow[0:1, 512 * c:512 * (c + 1)],
            start=True, stop=True)))(), r=["wrow", "ones"], w=["pb1"])
        add("vector", (lambda c=c: (lambda v: v.tensor_copy(
            wrdbc[:, 512 * c:512 * (c + 1)], pb[1][:, 0:512])))(), r=["pb1"],
            w=["wbc0" if T >= 1024 else "wrdbc"])

    def load_w_quarter(dW, q, tag):
        qb_ = 2048 * (q % 2)
        add("gpsimd", (lambda dW=dW, q=q, qb_=qb_: (lambda g: g.dma_start(
            out=ovl[:, qb_:qb_ + 2048].rearrange("p (a b) -> p a b", a=2),
            in_=dW.rearrange("(t p) d -> p t d", p=128)[:, 2 * q:2 * (q + 1), :])))(),
            w=[f"wn{q % 2}"], dma=True, tag=tag)

    def transpose_w_quarter(q, tag):
        opst = ovl.ap[0][0]
        qb_ = 2048 * (q % 2)
        for t_il in range(2):
            t_i = 2 * q + t_il
            for g4 in range(2):
                par = (t_il * 2 + g4) % 2
                dst = pbT if par == 0 else pbU
                dkey = "pbT" if par == 0 else "pbU"
                for t_d4 in range(4):
                    t_d = 4 * g4 + t_d4
                    add("tensor", (lambda t_il=t_il, t_d=t_d, t_d4=t_d4, qb_=qb_, dst=dst: (lambda t: t.transpose(
                        dst[:, 128 * t_d4:128 * (t_d4 + 1)],
                        ovl[:, qb_ + 1024 * t_il + 128 * t_d:qb_ + 1024 * t_il + 128 * (t_d + 1)],
                        ident[:])))(), r=[f"wn{q % 2}", "ident"], w=[dkey])
                if par == 0:
                    add("scalar", (lambda t_i=t_i, g4=g4, opst=opst: (lambda s: s.copy(
                        bass.AP(ovl.tensor, WT_OFF + 1024 * 4 * g4 + 128 * t_i,
                                [[opst, 128], [1024, 4], [1, 128]]),
                        pbT[:, 0:512].rearrange("p (a b) -> p a b", a=4))))(),
                        r=["pbT"], w=[wtI(t_i)])
                else:
                    add("vector", (lambda t_i=t_i, g4=g4, opst=opst: (lambda v: v.tensor_copy(
                        bass.AP(ovl.tensor, WT_OFF + 1024 * 4 * g4 + 128 * t_i,
                                [[opst, 128], [1024, 4], [1, 128]]),
                        pbU[:, 0:512].rearrange("p (a b) -> p a b", a=4))))(),
                        r=["pbU"], w=[wtI(t_i)])

    def gen_qk_one(dstoff, dst_keys_fn, bcol, nck, it, bank, eng):
        pkey = f"pb{bank}"
        for td in range(8):
            add("tensor", (lambda it=it, nck=nck, td=td, bank=bank: (lambda t: t.matmul(
                pb[bank][:, 0:512],
                ovl[:, WT_OFF + 1024 * td + 128 * it: WT_OFF + 1024 * td + 128 * (it + 1)],
                xnT[:, T * td + 512 * nck: T * td + 512 * (nck + 1)],
                start=(td == 0), stop=(td == 7))))(), r=[wtI(it), "xnT"], w=[pkey])
        if eng == "scalar":
            add("scalar", (lambda it=it, nck=nck, bank=bank, dstoff=dstoff, bcol=bcol: (lambda s: s.activation(
                bass.AP(big.tensor, dstoff + T * it + 512 * nck, [[PSTEP_B, 128], [1, 512]]),
                pb[bank][:, 0:512], AF.Identity,
                bias=bqkT[:, bcol + it:bcol + it + 1], scale=1.0)))(),
                r=[pkey, "bqkT"], w=list(dst_keys_fn(it)))
        else:
            add("vector", (lambda it=it, nck=nck, bank=bank, dstoff=dstoff, bcol=bcol: (lambda v: v.tensor_tensor(
                bass.AP(big.tensor, dstoff + T * it + 512 * nck, [[PSTEP_B, 128], [1, 512]]),
                pb[bank][:, 0:512],
                bass.AP(bqkT.tensor, bcol + it, [[bqkT.ap[0][0], 128], [0, 512]]),
                ALU.add)))(), r=[pkey, "bqkT"], w=list(dst_keys_fn(it)))

    def gen_qkt_nck(dstoff, dst_keys_fn, bcol, nck):
        for it in range(8):
            gen_qk_one(dstoff, dst_keys_fn, bcol, nck, it,
                       [5, 4, 3, 2][it % 4], "scalar" if it % 2 == 0 else "vector")

    def gen_kt_itpair(it0):
        for it in (it0, it0 + 1):
            for nck in range(NC512):
                gen_qk_one(KT_OFF, kt_k, 8, nck, it,
                           [5, 4, 3, 2][nck % 4], "scalar" if nck % 2 == 0 else "vector")

    def vgen_one(ntl, ih):
        wkeys = [wtI(4 * ih + j) for j in range(4)]
        bank = [5, 4, 3, 2][(ntl + 16 * ih) % 4]
        pkey = f"pb{bank}"
        for td in range(8):
            add("tensor", (lambda ntl=ntl, ih=ih, td=td, bank=bank: (lambda t: t.matmul(
                pb[bank][:, 0:512],
                xnT[:, T * td + 128 * ntl: T * td + 128 * (ntl + 1)],
                ovl[:, WT_OFF + 1024 * td + 512 * ih: WT_OFF + 1024 * td + 512 * (ih + 1)],
                start=(td == 0), stop=(td == 7))))(), r=wkeys + ["xnT"], w=[pkey])
        add("vector", (lambda ntl=ntl, ih=ih, bank=bank: (lambda v: v.tensor_tensor(
            bass.AP(big.tensor, V_OFF + 66 * NH * ntl + 66 * 8 * ih,
                    [[PSTEP_B, 128], [66, 8], [1, 64]]),
            pb[bank].rearrange("p (h d) -> p h d", h=8),
            bass.AP(a_nat.tensor, 512 * ih, [[a_nat.ap[0][0], 128], [64, 8], [1, 64]]),
            ALU.add)))(), r=[pkey, "a_nat"], w=list(v_k(ntl)))

    def load_transpose_w(dW, tag):
        load_w_quarter(dW, 0, tag)
        load_w_quarter(dW, 1, tag)
        transpose_w_quarter(0, tag)
        load_w_quarter(dW, 2, tag)
        transpose_w_quarter(1, tag)
        load_w_quarter(dW, 3, tag)
        transpose_w_quarter(2, tag)
        transpose_w_quarter(3, tag)

    add("gpsimd", lambda g: g.dma_start(out=bias_qkv[:], in_=dbq), w=["bias_qkv"], dma=True)
    load_transpose_w(dWq, "Wq")
    add("gpsimd", lambda g: g.dma_start(out=bias_qkv2[:], in_=dbk), w=["bias_qkv2"], dma=True)
    for bi_, (brow, brk) in enumerate(((bias_qkv, "bias_qkv"), (bias_qkv2, "bias_qkv2"))):
        for it in range(8):
            add("tensor", (lambda bi_=bi_, it=it, brow=brow, brk=brk: (lambda t: t.matmul(
                pb[1][:, 8 * bi_ + it:8 * bi_ + it + 1], brow[0:1, 128 * it:128 * (it + 1)],
                ones[0:1, 0:1], start=True, stop=True)))(), r=[brk, "ones"], w=["pb1"])
    add("vector", lambda v: v.tensor_copy(bqkT[:, 0:16], pb[1][:, 0:16]), r=["pb1"], w=["bqkT"])
    load_w_quarter(dWk, 0, "Wk")

    xslots = [(stx[0], ["stx0"]), (_sto, ["sto0", "sto1"]), (stxr, ["sx0", "sx1"])]
    nb = [big[:, V_OFF + 1024 * j: V_OFF + 1024 * (j + 1)] for j in range(4)]

    def p0_s1(nt):
        sx, kxl = xslots[nt % 3]
        xnb_t = nb[nt % 4]
        kb = f"nrm{nt % 4}"
        sb_ = 20 * (nt % 4)
        ks = lambda i: f"st{i}:{nt % 4}"
        stc = lambda i, n=1: stat[:, sb_ + i:sb_ + i + n]
        add("sync", (lambda nt=nt, sx=sx: (lambda s: s.dma_start(
            out=sx[:, 0:D], in_=dx[128 * nt:128 * (nt + 1), :])))(), w=kxl, dma=True)
        add("vector", (lambda sx=sx, stc=stc: (lambda v: v.bn_stats(
            stc(0, 6), sx[:, 0:512])))(), r=kxl, w=[ks(0)])
        add("vector", (lambda sx=sx, stc=stc: (lambda v: v.bn_stats(
            stc(6, 6), sx[:, 512:1024])))(), r=kxl, w=[ks(106)])
        add("vector", (lambda stc=stc: (lambda v: v.bn_aggr(stc(12, 2), stc(0, 12))))(),
            r=[ks(0), ks(106)], w=[ks(6)])
        add("scalar", (lambda stc=stc: (lambda s: s.activation(stc(14), stc(13), AF.Ln, bias=kons[:, 0:1], scale=1.0)))(),
            r=[ks(6), "kons"], w=[ks(14)])
        add("scalar", (lambda stc=stc: (lambda s: s.activation(stc(14), stc(14), AF.Exp, scale=-0.5)))(),
            r=[ks(14)], w=[ks(14) + "b"])
        add("vector", (lambda stc=stc: (lambda v: v.scalar_tensor_tensor(
            out=stc(15), in0=stc(12), scalar=-1.0, in1=stc(14), op0=ALU.mult, op1=ALU.mult)))(),
            r=[ks(6), ks(14) + "b"], w=[ks(15)])
        add("scalar", (lambda sx=sx, stc=stc, xnb_t=xnb_t, kxl=kxl: (lambda s: s.activation(
            xnb_t[:], sx[:, 0:D], AF.Identity, bias=stc(15), scale=stc(14))))(),
            r=kxl + [ks(15), ks(14) + "b"], w=[kb])

    def p0_s2(nt):
        xnb_t = nb[nt % 4]
        kb = f"nrm{nt % 4}"
        for td in range(8):
            add("tensor", (lambda td=td, xnb_t=xnb_t, kb=kb: (lambda t: t.transpose(
                pbT[:, 128 * td:128 * (td + 1)], xnb_t[:, 128 * td:128 * (td + 1)], ident[:])))(),
                r=[kb, "ident"], w=["pbT"])
        add("vector", (lambda nt=nt: (lambda v: v.tensor_copy(
            bass.AP(xnT.tensor, 128 * nt, [[xnT.ap[0][0], 128], [T, 8], [1, 128]]),
            pbT.rearrange("p (t c) -> p t c", t=8))))(), r=["pbT"], w=["xnT"])
        if nt % 4 == 3:
            for nt2 in range(nt - 3, nt + 1):
                for td in range(8):
                    for hl, wv, wk_ in ((0, wrdH, "wrdH"), (1, wrdL, "wrdL")):
                        add("tensor", (lambda nt2=nt2, td=td, wv=wv, hl=hl: (lambda t: t.matmul(
                            pb[1][:, nt2:nt2 + 1],
                            xnT[:, T * td + 128 * nt2: T * td + 128 * (nt2 + 1)],
                            wv[:, td:td + 1], start=(td == 0 and hl == 0), stop=(td == 7 and hl == 1))))(),
                            r=["xnT", wk_], w=["pb1"])
            add("vector", (lambda nt=nt: (lambda v: v.tensor_copy(
                dhi[:, nt - 3:nt + 1], pb[1][:, nt - 3:nt + 1])))(), r=["pb1"],
                w=[f"dhi:{i}" for i in range(nt - 3, nt + 1)])
            gen_qkt_nck(QT_OFF, qt_k, 0, nt // 4)

    for nt in range(NT):
        if nt >= 4:
            p0_s2(nt - 4)
        p0_s1(nt)
    for nt in range(NT - 4, NT):
        p0_s2(nt)
    add("vector", lambda v: v.memset(vones, 1.0),
        w=list(big_keys(V_OFF, V_OFF + 66 * NH * NT)) + [f"nrm{j}" for j in range(4)])

    load_w_quarter(dWk, 1, "Wk")
    transpose_w_quarter(0, "WkT")
    load_w_quarter(dWk, 2, "Wk")
    gen_kt_itpair(0)
    transpose_w_quarter(1, "WkT")
    load_w_quarter(dWk, 3, "Wk")
    gen_kt_itpair(2)
    transpose_w_quarter(2, "WkT")
    load_w_quarter(dWv, 0, "Wv")
    gen_kt_itpair(4)
    transpose_w_quarter(3, "WkT")
    load_w_quarter(dWv, 1, "Wv")
    gen_kt_itpair(6)
    add("gpsimd", lambda g: g.dma_start(out=bias_qkv[:], in_=dbv), w=["bias_qkv"], dma=True)
    transpose_w_quarter(0, "WvT")
    load_w_quarter(dWv, 2, "Wv")
    transpose_w_quarter(1, "WvT")
    load_w_quarter(dWv, 3, "Wv")
    for ih in range(2):
        add("tensor", (lambda ih=ih: (lambda t: t.matmul(
            pb[1][:, 0:512], ones[0:1, 0:128],
            bias_qkv[0:1, 512 * ih:512 * (ih + 1)], start=True, stop=True)))(),
            r=["bias_qkv", "ones"], w=["pb1"])
        add("vector", (lambda ih=ih: (lambda v: v.tensor_copy(
            a_nat[:, 512 * ih:512 * (ih + 1)], pb[1][:, 0:512])))(), r=["pb1"], w=["a_nat"])
    for ntl in range(8):
        vgen_one(ntl, 0)
    transpose_w_quarter(2, "WvT")
    load_w_quarter(dWo, 0, "Wo")
    for ntl in range(8, NT):
        vgen_one(ntl, 0)
    transpose_w_quarter(3, "WvT")
    load_w_quarter(dWo, 1, "Wo")
    for ntl in range(8):
        vgen_one(ntl, 1)
    transpose_w_quarter(0, "WoT")
    load_w_quarter(dWo, 2, "Wo")
    for ntl in range(8, NT):
        vgen_one(ntl, 1)
    transpose_w_quarter(1, "WoT")
    load_w_quarter(dWo, 3, "Wo")
    transpose_w_quarter(2, "WoT")
    transpose_w_quarter(3, "WoT")
    add("gpsimd", lambda g: g.dma_start(out=bias_o[:], in_=dbo), w=["bias_o"], dma=True)

    RT = NT
    dhi_keys = [f"dhi:{i}" for i in range(NT)]
    add("vector", lambda v: v.tensor_copy(rtr[:, 0:RT], dhi[:, 0:RT]),
        r=dhi_keys, w=["delta"])
    add("scalar", lambda s: s.activation(rtr[:, 16:16 + RT], rtr[:, 0:RT], AF.Exp, scale=-1.0),
        r=["delta"], w=["re"])
    add("vector", lambda v: v.tensor_scalar_add(rtr[:, 16:16 + RT], rtr[:, 16:16 + RT], 1.0), r=["re"], w=["re1"])
    add("vector", lambda v: v.reciprocal(rtr[:, 32:32 + RT], rtr[:, 16:16 + RT]), r=["re1"], w=["s0"])
    add("vector", lambda v: v.tensor_scalar(out=rtr[:, 48:48 + RT], in0=rtr[:, 0:RT], scalar1=thrc[:, 0:1],
                                            scalar2=None, op0=ALU.is_ge), r=["delta", "thrc"], w=["h0"])
    add("vector", lambda v: v.tensor_add(rtr[:, 64:64 + RT], rtr[:, 48:48 + RT], rtr[:, 32:32 + RT]),
        r=["h0", "s0"], w=["t0"])
    add("vector", lambda v: v.tensor_sub(rtr[:, 64:64 + RT], rtr[:, 64:64 + RT], rtr[:, 32:32 + RT]),
        r=["t0", "s0"], w=["t0b"])
    add("vector", lambda v: v.tensor_mul(w0c[:, 0:RT], rtr[:, 48:48 + RT], rtr[:, 64:64 + RT]),
        r=["h0", "t0b"], w=["w0c"])
    add("vector", lambda v: v.tensor_scalar(out=rtr[:, 48:48 + RT], in0=rtr[:, 48:48 + RT], scalar1=-1.0,
                                            scalar2=1.0, op0=ALU.mult, op1=ALU.add), r=["h0"], w=["h1"])
    add("vector", lambda v: v.tensor_scalar(out=rtr[:, 32:32 + RT], in0=rtr[:, 32:32 + RT], scalar1=-1.0,
                                            scalar2=1.0, op0=ALU.mult, op1=ALU.add), r=["s0"], w=["s1"])
    add("vector", lambda v: v.tensor_add(rtr[:, 64:64 + RT], rtr[:, 48:48 + RT], rtr[:, 32:32 + RT]),
        r=["h1", "s1"], w=["t1"])
    add("vector", lambda v: v.tensor_sub(rtr[:, 64:64 + RT], rtr[:, 64:64 + RT], rtr[:, 32:32 + RT]),
        r=["t1", "s1"], w=["t1b"])
    add("vector", lambda v: v.tensor_mul(w1c[:, 0:RT], rtr[:, 48:48 + RT], rtr[:, 64:64 + RT]),
        r=["h1", "t1b"], w=["w1c"])
    for (wc, wrk, bco, bck) in ((w0c, "w0c", wbc[0], "wbc0"), (w1c, "w1c", wbc[1], "wbc1")):
        add("vector", (lambda wc=wc: (lambda v: v.tensor_copy(wcb[:, 0:NT], wc[:, 0:NT])))(),
            r=[wrk], w=["wcb"])
        for c in range(T // 512):
            for k4 in range(4):
                nt = 4 * c + k4
                add("tensor", (lambda nt=nt, k4=k4: (lambda t: t.transpose(
                    pbT[0:1, 128 * k4:128 * (k4 + 1)], wcb[:, nt:nt + 1], ident[:])))(),
                    r=["wcb", "ident"], w=["pbT"])
            add("vector", lambda v: v.tensor_copy(wrow[0:1, 0:512], pbT[0:1, 0:512]),
                r=["pbT"], w=["wrow"])
            add("tensor", (lambda c=c: (lambda t: t.matmul(
                pb[1][:, 0:512], ones[0:1, 0:128], wrow[0:1, 0:512],
                start=True, stop=True)))(), r=["wrow", "ones"], w=["pb1"])
            add("vector", (lambda bco=bco, c=c, bck=bck: (lambda v: v.tensor_copy(
                bco[:, 512 * c:512 * (c + 1)], pb[1][:, 0:512])))(), r=["pb1"], w=[bck])

    SBANKS = (0, 1, 5)
    def qb_width(qb):
        return 129 if qb == 0 else (257 if qb == 1 else 385)

    def gu_preload(dW_, f_, tag=""):
        sg = stg[f_ % 4]
        gk = f"stg{f_ % 4}"
        add("gpsimd", (lambda dW_=dW_, f_=f_, sg=sg: (lambda g: g.dma_start(
            out=sg.rearrange("p (t f) -> p t f", t=8),
            in_=dW_.rearrange("(t p) f -> p t f", p=128)[:, :, 128 * f_:128 * (f_ + 1)])))(),
            w=[gk], dma=True, tag=tag)

    add("vector", lambda v: v.memset(r0tmp[:], 0.0), w=["r0tmp"])
    for dh in range(2):
        add("tensor", (lambda dh=dh: (lambda t: t.matmul(
            pb[1][:, 0:512], ones[0:1, 0:128],
            bias_o[0:1, 512 * dh:512 * (dh + 1)], start=True, stop=True)))(),
            r=["bias_o", "ones"], w=["pb1"])
        add("vector", (lambda dh=dh: (lambda v: v.tensor_copy(
            xnb[:, 512 * dh:512 * (dh + 1)], pb[1][:, 0:512])))(), r=["pb1"], w=["xnb"])
    qb_order = list(range(1, NT)) + [0]
    pending_tail = None
    for qi, qb in enumerate(qb_order):
        if qb == 0:
            for pq in (1, 2):
                add("sync", (lambda pq=pq: (lambda sy: sy.dma_start(
                    out=stg[pq % 2][:], in_=dascr[128 * pq:128 * (pq + 1), :])))(),
                    r=[f"ascr:{pq}"], w=[f"stg{pq % 2}"], dma=True)
        Wd_ = qb_width(qb)
        msk = mskA if qb == 0 else (mskB if qb == 1 else mskC)
        mkey = "mskA" if qb == 0 else ("mskB" if qb == 1 else "mskC")
        if qb == 0:
            blocks = [(0, 0)]
        elif qb == 1:
            blocks = [(0, 0), (128, 1)]
        else:
            blocks = [(0, 0), (128, qb - 1), (256, qb)]
        cur_kt = qb

        def produce_pair(tpr):
            h0, h1 = 2 * tpr, 2 * tpr + 1
            for bi, (co, kt) in enumerate(blocks):
                for h in (h0, h1):
                    sb = SBANKS[h % 3]
                    pbase = 64 * (h % 2)
                    tih = h // 2
                    add("tensor", (lambda co=co, kt=kt, sb=sb, pbase=pbase, tih=tih, qb=qb: (lambda t: t.matmul(
                        pb[sb][:, co:co + 128],
                        big[pbase:pbase + 64, KT_OFF + T * tih + 128 * kt: KT_OFF + T * tih + 128 * (kt + 1)],
                        big[pbase:pbase + 64, QT_OFF + T * tih + 128 * qb: QT_OFF + T * tih + 128 * (qb + 1)],
                        start=True, stop=True, tile_position=(pbase, 0))))(),
                        r=list(kt_k(tih)) + list(qt_k(tih)), w=[f"pb{sb}"])
            for h in (h0, h1):
                sb = SBANKS[h % 3]
                pbase = 64 * (h % 2)
                tih = h // 2
                add("tensor", (lambda sb=sb, pbase=pbase, tih=tih, cur_kt=cur_kt, Wd_=Wd_: (lambda t: t.matmul(
                    pb[sb][:, Wd_ - 1:Wd_],
                    big[pbase:pbase + 64, KT_OFF + T * tih + 128 * cur_kt: KT_OFF + T * tih + 128 * (cur_kt + 1)],
                    big[pbase:pbase + 64, QT_OFF + T * tih: QT_OFF + T * tih + 1],
                    start=True, stop=True, tile_position=(pbase, 0))))(),
                    r=list(kt_k(tih)) + list(qt_k(tih)), w=[f"pb{sb}"])
            for h in (h0, h1):
                sb = SBANKS[h % 3]
                add("scalar", (lambda sb=sb, h=h, Wd_=Wd_: (lambda s: s.activation(
                    exT[h % 6][:, 0:Wd_], pb[sb][:, 0:Wd_], AF.Exp, scale=0.125)))(),
                    r=[f"pb{sb}"], w=[f"exT{h % 6}"])
                meng = "vector" if h % 2 == 0 else "gpsimd"
                if qb == 1:
                    add(meng, (lambda h=h: (lambda v: v.tensor_mul(
                        exT[h % 6][:, 0:128], exT[h % 6][:, 0:128], mskB[:, 0:128])))(),
                        r=[f"exT{h % 6}", "mskB"], w=[f"exT{h % 6}"])
                    add(meng, (lambda h=h: (lambda v: v.tensor_mul(
                        exT[h % 6][:, 128:257], exT[h % 6][:, 128:257], mskC[:, 256:385])))(),
                        r=[f"exT{h % 6}", "mskC"], w=[f"exT{h % 6}"])
                else:
                    add(meng, (lambda h=h, msk=msk, Wd_=Wd_: (lambda v: v.tensor_mul(
                        exT[h % 6][:, 0:Wd_], exT[h % 6][:, 0:Wd_], msk[:, 0:Wd_])))(),
                        r=[f"exT{h % 6}", mkey], w=[f"exT{h % 6}"])

        def consume_pair(tpr):
            for h in (2 * tpr, 2 * tpr + 1):
                ob = 2 + (h // 7)
                okey = f"pb{ob}"
                ocol = 65 * (h % 7)
                for bi, (co, kt) in enumerate(blocks):
                    add("tensor", (lambda co=co, kt=kt, bi=bi, h=h, ob=ob, ocol=ocol, nb=len(blocks): (lambda t: t.matmul(
                        pb[ob][:, ocol:ocol + 65], exT[h % 6][:, co:co + 128],
                        big[:, V_OFF + 66 * NH * kt + 66 * h: V_OFF + 66 * NH * kt + 66 * h + 65],
                        start=(bi == 0), stop=(bi == nb - 1))))(),
                        r=[f"exT{h % 6}"] + list(v_k(kt)), w=[okey])
                if h < 15:
                    rb_, rc_ = 32 * (h // 5), 130 + 65 * (h % 5)
                else:
                    rb_, rc_ = 0, 130
                add("tensor", (lambda h=h, rb_=rb_, rc_=rc_, cur_kt=cur_kt, Wd_=Wd_: (lambda t: t.matmul(
                    pb[4][rb_:rb_ + 1, rc_:rc_ + 65],
                    exT[h % 6][:, Wd_ - 1:Wd_],
                    big[:, V_OFF + 66 * NH * cur_kt + 66 * h: V_OFF + 66 * NH * cur_kt + 66 * h + 65],
                    start=True, stop=True)))(), r=[f"exT{h % 6}"] + list(v_k(cur_kt)), w=["pb4"])
            if tpr == 0:
                add("vector", lambda v: v.tensor_add(
                    r0tmp[0:1, 0:65], r0tmp[0:1, 0:65], pb[4][0:1, 130:195]),
                    r=["pb4", "r0tmp"], w=["r0tmp"])

        def evac_bank(ob, nh_, rb0):
            add("vector", (lambda ob=ob, nh_=nh_, rb0=rb0: (lambda v: v.reciprocal(
                rsb[:, rb0:rb0 + nh_],
                bass.AP(pb[ob].tensor, pb[ob].offset + 64, [[pb[ob].ap[0][0], 128], [65, nh_]]))))(),
                r=[f"pb{ob}"], w=["rsb"])
            add("vector", (lambda ob=ob, nh_=nh_, rb0=rb0: (lambda v: v.tensor_tensor(
                bass.AP(a_nat.tensor, 64 * rb0, [[a_nat.ap[0][0], 128], [64, nh_], [1, 64]]),
                bass.AP(pb[ob].tensor, pb[ob].offset, [[pb[ob].ap[0][0], 128], [65, nh_], [1, 64]]),
                bass.AP(rsb.tensor, rb0, [[rsb.ap[0][0], 128], [1, nh_], [0, 64]]),
                ALU.mult)))(), r=[f"pb{ob}", "rsb"], w=["a_nat"])

        produce_pair(0)
        produce_pair(1)
        produce_pair(2)
        if pending_tail is not None:
            pending_tail()
            pending_tail = None
        for tpr in range(3, 8):
            consume_pair(tpr - 3)
            if tpr == 7:
                evac_bank(2, 7, 0)
            produce_pair(tpr)
        consume_pair(5)
        consume_pair(6)
        evac_bank(3, 7, 7)
        consume_pair(7)
        evac_bank(4, 2, 14)
        add("vector", lambda v: v.tensor_add(
            r0tmp[0:1, 65:325], r0tmp[0:1, 65:325], pb[4][0:1, 195:455]),
            r=["pb4", "r0tmp"], w=["r0tmp"])
        add("vector", lambda v: v.tensor_add(
            r0tmp[0:1, 325:390], r0tmp[0:1, 325:390], pb[4][0:1, 130:195]),
            r=["pb4", "r0tmp"], w=["r0tmp"])
        for rb_ in (32, 64):
            add("vector", (lambda rb_=rb_: (lambda v: v.tensor_add(
                r0tmp[rb_:rb_ + 1, 0:325], r0tmp[rb_:rb_ + 1, 0:325],
                pb[4][rb_:rb_ + 1, 130:455])))(), r=["pb4", "r0tmp"], w=["r0tmp"])
        if qb == 0:
            for rb_, ncnt in ((0, 6), (32, 5), (64, 5)):
                add("vector", (lambda rb_=rb_, ncnt=ncnt: (lambda v: v.reciprocal(
                    bass.AP(r0tmp.tensor, 468 * rb_ + 420, [[468, 1], [1, ncnt]]),
                    bass.AP(r0tmp.tensor, 468 * rb_ + 64, [[468, 1], [65, ncnt]]))))(),
                    r=["r0tmp"], w=["r0tmp"])
                add("vector", (lambda rb_=rb_, ncnt=ncnt: (lambda v: v.tensor_tensor(
                    bass.AP(r0bf.tensor, r0bf.ap[0][0] * rb_, [[r0bf.ap[0][0], 1], [64, ncnt], [1, 64]]),
                    bass.AP(r0tmp.tensor, 468 * rb_, [[468, 1], [65, ncnt], [1, 64]]),
                    bass.AP(r0tmp.tensor, 468 * rb_ + 420, [[468, 1], [1, ncnt], [0, 64]]), ALU.mult)))(),
                    r=["r0tmp"], w=["r0bf"])
            add("sync", lambda sy: sy.dma_start(
                out=a_nat[0:1, 0:320], in_=r0bf[0:1, 0:320]),
                r=["r0bf"], w=["a_nat"], dma=True)
            add("sync", lambda sy: sy.dma_start(
                out=a_nat[0:1, 960:1024], in_=r0bf[0:1, 320:384]),
                r=["r0bf"], w=["a_nat"], dma=True)
            add("sync", lambda sy: sy.dma_start(
                out=a_nat[0:1, 320:640], in_=r0bf[32:33, 0:320]),
                r=["r0bf"], w=["a_nat"], dma=True)
            add("sync", lambda sy: sy.dma_start(
                out=a_nat[0:1, 640:960], in_=r0bf[64:65, 0:320]),
                r=["r0bf"], w=["a_nat"], dma=True)
        def make_tail(qb=qb):
            def tail():
                for g4 in range(2):
                    dst = pbT if g4 == 0 else pbU
                    dkey = "pbT" if g4 == 0 else "pbU"
                    for t4 in range(4):
                        ti = 4 * g4 + t4
                        add("tensor", (lambda ti=ti, t4=t4, dst=dst, dkey=dkey: (lambda t: t.transpose(
                            dst[:, 128 * t4:128 * (t4 + 1)],
                            a_nat[:, 128 * ti:128 * (ti + 1)], ident[:])))(),
                            r=["a_nat", "ident"], w=[dkey])
                    if g4 == 0:
                        add("scalar", (lambda g4=g4: (lambda s: s.copy(
                            bass.AP(a_T.tensor, 512 * g4, [[a_T.ap[0][0], 128], [128, 4], [1, 128]]),
                            pbT[:, 0:512].rearrange("p (a b) -> p a b", a=4))))(),
                            r=["pbT"], w=["a_T"])
                    else:
                        add("vector", (lambda g4=g4: (lambda v: v.tensor_copy(
                            bass.AP(a_T.tensor, 512 * g4, [[a_T.ap[0][0], 128], [128, 4], [1, 128]]),
                            pbU[:, 0:512].rearrange("p (a b) -> p a b", a=4))))(),
                            r=["pbU"], w=["a_T"])
                if qb != 0:
                    add("sync", (lambda qb=qb: (lambda s: s.dma_start(
                        out=dascr[128 * qb:128 * (qb + 1), :], in_=a_T[:])))(),
                        r=["a_T"], w=[f"ascr:{qb}"], dma=True)
            return tail
        pending_tail = make_tail()

    gu_preload(dWg[0], 2, "g0pre")
    gu_preload(dWg[0], 3, "g0pre")
    for qb in list(range(1, NT)):
        sg = stg[qb % 2]
        gk = f"stg{qb % 2}"
        if qb > 2:
            add("sync", (lambda qb=qb, sg=sg: (lambda s: s.dma_start(
                out=sg[:], in_=dascr[128 * qb:128 * (qb + 1), :])))(),
                r=[f"ascr:{qb}"], w=[gk], dma=True)
        for dh in range(2):
            bank = [5, 4, 3, 2][(2 * (qb - 1) + dh) % 4]
            pkey = f"pb{bank}"
            for ti in range(8):
                add("tensor", (lambda ti=ti, dh=dh, bank=bank, sg=sg: (lambda t: t.matmul(
                    pb[bank][:, 0:512], sg[:, 128 * ti:128 * (ti + 1)],
                    ovl[:, WT_OFF + 1024 * ti + 512 * dh:WT_OFF + 1024 * ti + 512 * (dh + 1)],
                    start=(ti == 0), stop=(ti == 7))))(),
                    r=[gk] + [wtI(4 * dh + j) for j in range(4)], w=[pkey])
            add("vector", (lambda dh=dh, bank=bank: (lambda v: v.tensor_tensor(
                sto[dh][:], pb[bank][:, 0:512],
                xnb[:, 512 * dh:512 * (dh + 1)], ALU.add)))(),
                r=[pkey, "xnb"], w=[f"sto{dh}"])
            add("gpsimd", (lambda qb=qb, dh=dh: (lambda g: g.dma_start(
                out=dpscr[128 * qb:128 * (qb + 1), 512 * dh:512 * (dh + 1)],
                in_=sto[dh][:])))(), r=[f"sto{dh}"], w=[f"pscr:{qb}"], dma=True)
        if qb == 14:
            gu_preload(dWg[0], 0, "g0pre")
        if qb == 15:
            gu_preload(dWg[0], 1, "g0pre")

    pending_tail()
    pending_tail = None
    for dh in range(2):
        bank = [5, 4, 3, 2][(30 + dh) % 4]
        pkey = f"pb{bank}"
        for ti in range(8):
            add("tensor", (lambda ti=ti, dh=dh, bank=bank: (lambda t: t.matmul(
                pb[bank][:, 0:512], a_T[:, 128 * ti:128 * (ti + 1)],
                ovl[:, WT_OFF + 1024 * ti + 512 * dh:WT_OFF + 1024 * ti + 512 * (dh + 1)],
                start=(ti == 0), stop=(ti == 7))))(),
                r=["a_T"] + [wtI(4 * dh + j) for j in range(4)], w=[pkey])
        add("vector", (lambda dh=dh, bank=bank: (lambda v: v.tensor_tensor(
            sto[dh][:], pb[bank][:, 0:512],
            xnb[:, 512 * dh:512 * (dh + 1)], ALU.add)))(),
            r=[pkey, "xnb"], w=[f"sto{dh}"])
        add("gpsimd", (lambda dh=dh: (lambda g: g.dma_start(
            out=dpscr[0:128, 512 * dh:512 * (dh + 1)],
            in_=sto[dh][:])))(), r=[f"sto{dh}"], w=["pscr:0"], dma=True)


    def gu_G(dWg_, tag, skip_loads=0, inject=None, forder=None):
        for fi, f_ in enumerate(forder if forder is not None else range(16)):
            if inject and fi in inject:
                inject[fi]()
            sg = stg[f_ % 4]
            gk = f"stg{f_ % 4}"
            if fi >= skip_loads:
                gu_preload(dWg_, f_, tag)
            for ncs in range(NC512):
                bank = ncs % 2
                pkey = f"pb{bank}"
                for td in range(8):
                    add("tensor", (lambda f_=f_, ncs=ncs, td=td, bank=bank, sg=sg: (lambda t: t.matmul(
                        pb[bank][:, 0:512], sg[:, 128 * td:128 * (td + 1)],
                        xnT[:, T * td + 512 * ncs:T * td + 512 * (ncs + 1)],
                        start=(td == 0), stop=(td == 7))))(), r=[gk, "xnT"], w=[pkey])
                add("scalar", (lambda f_=f_, ncs=ncs, bank=bank: (lambda s: s.activation(
                    big[:, H_OFF + T * f_ + 512 * ncs:H_OFF + T * f_ + 512 * (ncs + 1)],
                    pb[bank][:, 0:512], AF.Silu)))(), r=[pkey], w=list(h_k(f_)))
    def gu_U(dWu_, e_idx, tag, inject=None):
        for f_ in range(16):
            if inject and f_ in inject:
                inject[f_]()
            sg = stg[f_ % 4]
            gk = f"stg{f_ % 4}"
            gu_preload(dWu_, f_, tag)
            for ncs in range(NC512):
                bank = 2 + ncs % 2
                pkey = f"pb{bank}"
                for td in range(8):
                    add("tensor", (lambda f_=f_, ncs=ncs, td=td, bank=bank, sg=sg: (lambda t: t.matmul(
                        pb[bank][:, 0:512], sg[:, 128 * td:128 * (td + 1)],
                        xnT[:, T * td + 512 * ncs:T * td + 512 * (ncs + 1)],
                        start=(td == 0), stop=(td == 7))))(), r=[gk, "xnT"], w=[pkey])
                add("vector", (lambda f_=f_, ncs=ncs, bank=bank: (lambda v: v.tensor_mul(
                    big[:, H_OFF + T * f_ + 512 * ncs:H_OFF + T * f_ + 512 * (ncs + 1)],
                    big[:, H_OFF + T * f_ + 512 * ncs:H_OFF + T * f_ + 512 * (ncs + 1)],
                    pb[bank][:, 0:512])))(), r=[pkey] + list(h_k(f_)), w=list(h_k(f_)))
                if e_idx is not None:
                    add("vector", (lambda f_=f_, ncs=ncs, e_idx=e_idx: (lambda v: v.tensor_mul(
                        big[:, H_OFF + T * f_ + 512 * ncs:H_OFF + T * f_ + 512 * (ncs + 1)],
                        big[:, H_OFF + T * f_ + 512 * ncs:H_OFF + T * f_ + 512 * (ncs + 1)],
                        wbc[e_idx][:, 512 * ncs:512 * (ncs + 1)])))(),
                        r=[f"wbc{e_idx}"] + list(h_k(f_)), w=list(h_k(f_)))

    def wdc_load(dWd_, dh, buf, c, tag):
        off = (WDC_OFF if buf == 0 else WDC2_OFF) + 2048 * c
        key = f"big:{off // 2048}"
        add("gpsimd", (lambda dWd_=dWd_, dh=dh, off=off, c=c: (lambda g: g.dma_start(
            out=bass.AP(big.tensor, off, [[PSTEP_B, 128], [512, 4], [1, 512]]),
            in_=dWd_.rearrange("(t p) d -> p t d", p=128)[:, 4 * c:4 * (c + 1), 512 * dh:512 * (dh + 1)])))(),
            w=[key], dma=True, tag=tag)

    def d2_loads(ntile, dh):
        sl = ntile % 4
        sxl = ntile % 2
        sx = stxr[:, 512 * sxl:512 * (sxl + 1)]
        add("scalar", (lambda ntile=ntile, dh=dh, sx=sx: (lambda s: s.dma_start(
            out=sx,
            in_=dx[128 * ntile:128 * (ntile + 1), 512 * dh:512 * (dh + 1)])))(),
            w=[f"sx{sxl}"], dma=True)
        pp = stg[sl][:, 0:512]
        add("scalar", (lambda ntile=ntile, dh=dh, pp=pp: (lambda s: s.dma_start(
            out=pp,
            in_=dpscr[128 * ntile:128 * (ntile + 1), 512 * dh:512 * (dh + 1)])))(),
            r=[f"pscr:{ntile}"], w=[f"stg{sl}"], dma=True)

    def d_stage_half(mode, dh, buf, tag):
        woff = WDC_OFF if buf == 0 else WDC2_OFF
        if mode == 2:
            for nt_ in range(2):
                d2_loads(nt_, dh)
        for g8 in range(T // 256):
            grp = g8 % 3
            banks = (2 * grp, 2 * grp + 1)
            for f_ in range(16):
                for sub in range(2):
                    ntile = 2 * g8 + sub
                    bkv = banks[sub]
                    add("tensor", (lambda f_=f_, ntile=ntile, bk=bkv, woff=woff: (lambda t: t.matmul(
                        pb[bk][:, 0:512],
                        big[:, H_OFF + T * f_ + 128 * ntile:H_OFF + T * f_ + 128 * (ntile + 1)],
                        big[:, woff + 512 * f_:woff + 512 * (f_ + 1)],
                        start=(f_ == 0), stop=(f_ == 15))))(),
                        r=list(h_k(f_)) + [f"big:{(woff + 512 * f_) // 2048}"], w=[f"pb{bkv}"])
            for sub in range(2):
                ntile = 2 * g8 + sub
                so = sto[ntile % 2]
                sok = f"sto{ntile % 2}"
                pkey = f"pb{banks[sub]}"
                if mode == 2:
                    sl = ntile % 4
                    sxl = ntile % 2
                    sx = stxr[:, 512 * sxl:512 * (sxl + 1)]
                    pp = stg[sl][:, 0:512]
                    add("vector", (lambda bk=banks[sub], so=so, sx=sx: (lambda v: v.tensor_add(
                        so[:], pb[bk][:, 0:512], sx)))(), r=[pkey, f"sx{sxl}"], w=[sok])
                    add("vector", (lambda so=so, pp=pp: (lambda v: v.tensor_add(
                        so[:], so[:], pp)))(), r=[sok, f"stg{sl}"], w=[sok])
                else:
                    add("vector", (lambda bk=banks[sub], so=so: (lambda v: v.tensor_copy(
                        so[:], pb[bk][:, 0:512])))(), r=[pkey], w=[sok])
                okey = f"out:{ntile}:{dh}"
                if mode == 0:
                    add("gpsimd", (lambda so=so, ntile=ntile, dh=dh: (lambda g: g.dma_start(
                        out=dout[128 * ntile:128 * (ntile + 1), 512 * dh:512 * (dh + 1)],
                        in_=so[:])))(), r=[sok, okey], w=[okey], dma=True, tag=tag)
                else:
                    add("gpsimd", (lambda so=so, ntile=ntile, dh=dh: (lambda g: g.dma_start(
                        out=dout[128 * ntile:128 * (ntile + 1), 512 * dh:512 * (dh + 1)],
                        in_=so[:], accum_op=ALU.add)))(), r=[sok, okey], w=[okey], dma=True, tag=tag)
            if mode == 2:
                for nt_ in (2 * g8 + 2, 2 * g8 + 3):
                    if nt_ < 16:
                        d2_loads(nt_, dh)

    def wdc_injects(dWd_, tag):
        inj = {}
        for c in range(4):
            inj[1 + 2 * c] = (lambda dWd_=dWd_, c=c: wdc_load(dWd_, 0, 0, c, tag + "a"))
            inj[8 + 2 * c] = (lambda dWd_=dWd_, c=c: wdc_load(dWd_, 1, 1, c, tag + "b"))
        return inj

    gu_G(dWg[0], "g0", skip_loads=4, forder=[2, 3, 0, 1] + list(range(4, 16)))
    gu_U(dWu[0], 0, "u0", inject=wdc_injects(dWd[0], "wd0"))
    for _pf in range(4):
        gu_preload(dWg[1], _pf, "pre1")
    d_stage_half(0, 0, 0, "d0h0")
    d_stage_half(0, 1, 1, "d0h1")
    gu_G(dWg[1], "g1", skip_loads=4)
    gu_U(dWu[1], 1, "u1", inject=wdc_injects(dWd[1], "wd1"))
    for _pf in range(4):
        gu_preload(dSg, _pf, "pres")
    d_stage_half(1, 0, 0, "d1h0")
    d_stage_half(1, 1, 1, "d1h1")
    gu_G(dSg, "gs", skip_loads=4)
    gu_U(dSu, None, "us", inject=wdc_injects(dSd, "sd"))
    d_stage_half(2, 0, 0, "dsh0")
    d_stage_half(2, 1, 1, "dsh1")

    B.finalize(nc)
    fc = B.final_counts
    B.emit(nc)
    with nc.Block() as blk:
        @blk.gpsimd
        def _(g):
            for e in ("gpsimd", "sync", "scalar"):
                n = B.qcnt[e]
                for sl in range(B.R):
                    full = 16 * ((n - sl + B.R - 1) // B.R) if n > 0 else 0
                    if full > 0:
                        g.wait_ge(B.sems[("q", e, sl)], full)
    return nc


def kernel(L0_attn_x, L0_attn_Wq, L0_attn_bq, L0_attn_Wk, L0_attn_bk, L0_attn_Wv,
           L0_attn_bv, L0_attn_Wo, L0_attn_bo, L0_ffn_W_router, L0_ffn_router_bias,
           L0_ffn_Wg, L0_ffn_Wu, L0_ffn_Wd, L0_ffn_Sg, L0_ffn_Su, L0_ffn_Sd):
    T = 2048
    x = np.ascontiguousarray(np.asarray(L0_attn_x, dtype=np.float32))
    Bsz = x.shape[0]
    mA, mB, mC = _masks()
    com = {
        "Wq": np.ascontiguousarray(L0_attn_Wq, dtype=np.float32),
        "bq": np.ascontiguousarray(np.asarray(L0_attn_bq, dtype=np.float32).reshape(1, D)),
        "Wk": np.ascontiguousarray(L0_attn_Wk, dtype=np.float32),
        "bk": np.ascontiguousarray(np.asarray(L0_attn_bk, dtype=np.float32).reshape(1, D)),
        "Wv": np.ascontiguousarray(L0_attn_Wv, dtype=np.float32),
        "bv": np.ascontiguousarray(np.asarray(L0_attn_bv, dtype=np.float32).reshape(1, D)),
        "Wo": np.ascontiguousarray(L0_attn_Wo, dtype=np.float32),
        "bo": np.ascontiguousarray(np.asarray(L0_attn_bo, dtype=np.float32).reshape(1, D)),
        "Wr": np.ascontiguousarray(L0_ffn_W_router, dtype=np.float32),
        "rb": np.ascontiguousarray(np.asarray(L0_ffn_router_bias, dtype=np.float32).reshape(1, NE)),
        "Wg0": np.ascontiguousarray(np.asarray(L0_ffn_Wg, dtype=np.float32)[0]),
        "Wg1": np.ascontiguousarray(np.asarray(L0_ffn_Wg, dtype=np.float32)[1]),
        "Wu0": np.ascontiguousarray(np.asarray(L0_ffn_Wu, dtype=np.float32)[0]),
        "Wu1": np.ascontiguousarray(np.asarray(L0_ffn_Wu, dtype=np.float32)[1]),
        "Wd0": np.ascontiguousarray(np.asarray(L0_ffn_Wd, dtype=np.float32)[0]),
        "Wd1": np.ascontiguousarray(np.asarray(L0_ffn_Wd, dtype=np.float32)[1]),
        "Sg": np.ascontiguousarray(L0_ffn_Sg, dtype=np.float32),
        "Su": np.ascontiguousarray(L0_ffn_Su, dtype=np.float32),
        "Sd": np.ascontiguousarray(L0_ffn_Sd, dtype=np.float32),
        "mA": mA, "mB": mB, "mC": mC,
        "ident": np.eye(128, dtype=np.float32),
    }
    nc = build_program(T)
    in_maps = []
    for b in range(Bsz):
        m = dict(com)
        m["x"] = np.ascontiguousarray(x[b])
        in_maps.append(m)
    res = run_bass_kernel_spmd(nc, in_maps, core_ids=list(range(Bsz)))
    out = np.stack([np.asarray(r["out"], dtype=np.float32) for r in res.results], axis=0)
    return out
